# version 14
# baseline (speedup 1.0000x reference)
import os
import numpy as np
import ml_dtypes
import concourse.bass as bass
import concourse.mybir as mybir
from concourse.bass_utils import run_bass_kernel_spmd
from concourse.ap import AP

F32 = mybir.dt.float32
BF16 = mybir.dt.bfloat16
FP8 = mybir.dt.float8e4
AF = mybir.ActivationFunctionType
ALU = mybir.AluOpType
AXX = mybir.AxisListType.X

T = 2048
C = 1024
NH = 16
NT = T // 128
PROW = 64 + T + 8
WROW = 256
NWBUF = 4

DEBUG = os.environ.get("KDEBUG") == "1"
PHASE_LIM = int(os.environ.get("KPHASE", "9"))


def mkap(t, offset, dims):
    return AP(t, int(offset), [list(d) for d in dims])


class Prog:
    ENGS = ("sync", "vector", "scalar", "tensor", "gpsimd")

    def __init__(self, nc):
        self.nc = nc
        self.plan = {e: [] for e in self.ENGS}
        self.sem = {}
        self.cnt = {}
        for name in ("SY", "GP", "PE", "DV", "AC", "CC",
                     "LDX0", "LDX1", "WW0", "WW1", "PQD0", "PQD1",
                     "SKW0", "SKW1", "STO", "BDL", "PT0", "PT1"):
            self.sem[name] = nc.alloc_semaphore(name)
            self.cnt[name] = 0

    def wait(self, eng, sname, val=None):
        v = self.cnt[sname] if val is None else val
        if v <= 0:
            return
        s = self.sem[sname]
        self.plan[eng].append(lambda e, s=s, v=v: e.wait_ge(s, v))

    def dma(self, dst, src, eng="sync", sem=None, **kw):
        sname = sem or ("GP" if eng == "gpsimd" else "SY")
        s = self.sem[sname]
        self.cnt[sname] += 16
        self.plan[eng].append(
            lambda e, d=dst, sr=src, s=s, kw=kw: e.dma_start(out=d, in_=sr, **kw).then_inc(s, 16)
        )
        return self.cnt[sname]

    CHAIN = {"vector": "DV", "scalar": "AC", "gpsimd": "CC"}

    def op(self, eng, meth, *args, inc=None, chain=True, **kw):
        sname = self.CHAIN.get(eng, inc)
        if sname is None:
            self.plan[eng].append(lambda e, m=meth, a=args, kw=kw: getattr(e, m)(*a, **kw))
            return None
        s = self.sem[sname]
        if eng in self.CHAIN and chain and self.cnt[sname] > 0:
            v = self.cnt[sname]
            self.plan[eng].append(lambda e, s=s, v=v: e.wait_ge(s, v))
        self.cnt[sname] += 1
        self.plan[eng].append(
            lambda e, m=meth, a=args, kw=kw, s=s: getattr(e, m)(*a, **kw).then_inc(s, 1)
        )
        return self.cnt[sname]

    def collective(self, kind, op, groups, ins, outs):
        s = self.sem["CC"]
        self.cnt["CC"] += 1
        self.plan["gpsimd"].append(
            lambda e, k=kind, o=op, g=groups, i=ins, u=outs, s=s:
                e.collective_compute(k, o, g, i, u).then_inc(s, 1)
        )
        return self.cnt["CC"]

    def emit(self):
        with self.nc.Block() as block:
            for ename in self.ENGS:
                items = self.plan[ename]
                getattr(block, ename)(lambda e, items=items: [f(e) for f in items])


def build_nc():
    nc = bass.Bass("TRN2", target_bir_lowering=False)

    x_in = nc.dram_tensor("x", [T, C], F32, kind="ExternalInput")
    w_disp = nc.dram_tensor("w_disp", [C, 256], F32, kind="ExternalInput")
    w_val = nc.dram_tensor("w_val", [C, C], F32, kind="ExternalInput")
    rel_pos = nc.dram_tensor("rel_pos", [64, 16], F32, kind="ExternalInput")
    w_strain = nc.dram_tensor("w_strain", [16, 16], F32, kind="ExternalInput")
    w_pos = nc.dram_tensor("w_pos", [16, 16], F32, kind="ExternalInput")
    w_bond = nc.dram_tensor("w_bond", [16], F32, kind="ExternalInput")
    w_dmg = nc.dram_tensor("w_dmg", [16, 16], F32, kind="ExternalInput")
    b_dmg = nc.dram_tensor("b_dmg", [16], F32, kind="ExternalInput")
    w_dmg_out = nc.dram_tensor("w_dmg_out", [16], F32, kind="ExternalInput")
    b_dmg_out = nc.dram_tensor("b_dmg_out", [1], F32, kind="ExternalInput")
    w_cproj = nc.dram_tensor("w_cproj", [C, C], F32, kind="ExternalInput")
    w_ffn = nc.dram_tensor("w_ffn", [C, C], F32, kind="ExternalInput")
    b_ffn = nc.dram_tensor("b_ffn", [C], F32, kind="ExternalInput")
    ident_in = nc.dram_tensor("ident", [128, 128], BF16, kind="ExternalInput")
    sel01_in = nc.dram_tensor("sel01", [128, 128], BF16, kind="ExternalInput")
    mask01_in = nc.dram_tensor("mask01", [128, 64], BF16, kind="ExternalInput")
    ones_in = nc.dram_tensor("onesrow", [1, 128], BF16, kind="ExternalInput")

    y_out = nc.dram_tensor("y", [T, C], F32, kind="ExternalOutput")

    p_dram = nc.dram_tensor("p_dram", [256 * PROW], BF16, kind="Internal")
    q_dram = nc.dram_tensor("q_dram", [256 * PROW], BF16, kind="Internal")
    w_dram = nc.dram_tensor("w_dram", [NWBUF * T * WROW], BF16, kind="Internal")
    x2_dram = nc.dram_tensor("x2_dram", [T * C], F32, kind="Internal")
    r_dram = nc.dram_tensor("r_dram", [64 * 16], F32, kind="Internal")
    sc_dram = nc.dram_tensor("sc_dram", [8], F32, kind="Internal")
    amax_dram = nc.dram_tensor("amax_dram", [128], F32, kind="Internal")
    cc_in = nc.dram_tensor("cc_in", [1, 128], F32, kind="Internal")
    cc_out = nc.dram_tensor("cc_out", [1, 128], F32, kind="Internal")

    dbg = {}
    if DEBUG:
        for nm, shp, dt in [
            ("p_fm", [128, 4096], BF16), ("q_fm", [128, 4096], BF16),
            ("val", [128, NT * 1024], BF16), ("yfm", [128, 16384], BF16),
            ("wun", [128, T], BF16), ("wtj", [128, 1024], BF16),
            ("attn", [128, 16384], BF16), ("x2d", [T * C], F32),
            ("xq", [128, 16384], BF16), ("wffnt", [128, 8192], BF16),
        ]:
            dbg[nm] = nc.dram_tensor("dbg_" + nm, shp, dt, kind="ExternalOutput")

    sb = nc.alloc_sbuf_tensor
    Wd = sb("Wd", [128, 8 * 256], BF16)
    Wv = sb("Wv", [128, 8 * 1024], BF16)
    Wc = sb("Wc", [128, 8 * 1024], BF16)
    BDs = sb("BDs", [128, 128], BF16)
    BDd = sb("BDd", [128, 128], BF16)
    wideP = sb("wideP", [128, 128], BF16)
    wideD = sb("wideD", [128, 128], BF16)
    sel01 = sb("sb_sel01", [128, 128], BF16)
    ident = sb("sb_ident", [128, 128], BF16)
    mask01 = sb("sb_mask01", [128, 64], BF16)
    onesr = sb("onesr", [1, 128], BF16)
    rcol = sb("rcol", [128, 8], F32)
    bdmgc = sb("bdmgc", [128, 1], F32)
    b0c = sb("b0c", [128, 1], F32)
    wbc = sb("wbc", [128, 1], F32)
    wdc = sb("wdc", [128, 1], F32)
    scal = sb("scal", [128, 4], F32)
    cm5 = sb("cm5", [128, 1], F32)
    s_tiny = sb("s_tiny", [1, 8], F32)

    xa = sb("xa", [128, 1024], F32)
    xb = sb("xb", [128, 1024], F32)
    y2scr = sb("y2scr", [128, 1024], F32)
    ytm = sb("ytm", [128, 1024], BF16)
    mcol = sb("mcol", [128, NT], F32)
    rstd = sb("rstd", [128, NT], F32)
    c1 = sb("c1", [128, 1], F32)
    c2 = sb("c2", [128, 1], F32)
    amx = sb("amx", [128, NT], F32)
    wamx = sb("wamx", [128, 8], F32)
    arow = sb("arow", [1, 128], F32)

    big_fm = sb("big_fm", [128, 16384], BF16)
    big_tm = sb("big_tm", [128, 16384], BF16)
    dspb = [sb(f"dspb{i}", [128, 512], BF16) for i in range(2)]
    pq_buf = [sb(f"pqb{i}", [128, 512], BF16) for i in range(2)]

    repP = [sb(f"repP{i}", [128, 2112], BF16) for i in range(2)]
    bcP = [sb(f"bcP{i}", [128, T], BF16) for i in range(2)]
    repQ = [sb(f"repQ{i}", [128, 2112], BF16) for i in range(2)]
    bcQ = [sb(f"bcQ{i}", [128, T], BF16) for i in range(2)]
    zsl = [sb(f"zsl{i}", [128, T], BF16) for i in range(2)]
    gsl = [sb(f"gsl{i}", [128, T], BF16) for i in range(2)]
    sgm = sb("sgm", [128, T], BF16)
    w_un = sb("w_un", [128, T], BF16)
    w_tj = [sb(f"wtjb{i}", [128, 1024], BF16) for i in range(2)]
    wsum = sb("wsum", [128, 16], F32)
    wrcp = sb("wrcp", [128, 16], F32)
    Vself = [sb(f"Vs{i}", [128, 2048], BF16) for i in range(2)]
    VcS = sb("VcS", [128, 2048], BF16)
    Vcorn = [VcS[0:64, :], VcS[64:128, :]]
    Wtil = sb("Wtil", [128, 3072], BF16)
    relT = sb("relT", [16, 64], F32)
    wpos_s = sb("wpos_s", [16, 16], F32)
    small = sb("small", [64, 16], F32)
    brow = sb("brow", [1, 1024], F32)
    brow_bf = sb("brow_bf", [1, 1024], BF16)

    psA = nc.alloc_psum_tensor("psA", [128, 512], F32)
    psB = nc.alloc_psum_tensor("psB", [128, 512], F32)
    ps_ctr = nc.alloc_psum_tensor("ps_ctr", [128, 2048], F32)
    ps_bf = nc.alloc_psum_tensor("ps_bf", [128, 1024], BF16)
    ps_w = nc.alloc_psum_tensor("ps_w", [128, 1024], BF16)

    P = Prog(nc)

    P.dma(ident[:], ident_in.ap())
    P.dma(sel01[:], sel01_in.ap())
    P.dma(mask01[:], mask01_in.ap())
    P.dma(onesr[:], ones_in.ap())
    P.dma(bdmgc[:], mkap(b_dmg, 0, [[0, 8], [1, 16], [0, 1]]))
    P.dma(wbc[:], mkap(w_bond, 0, [[0, 8], [1, 16], [0, 1]]))
    P.dma(wdc[:], mkap(w_dmg_out, 0, [[0, 8], [1, 16], [0, 1]]))
    P.dma(b0c[:], mkap(b_dmg_out, 0, [[0, 128], [0, 1]]))
    P.dma(relT[:], mkap(rel_pos, 0, [[1, 16], [16, 64]]), allow_slow_non_contiguous=True)
    P.dma(wpos_s[:], w_pos.ap())
    P.dma(brow[:], mkap(b_ffn, 0, [[0, 1], [1, C]]))
    P.dma(Wd[:].rearrange("p (c o) -> p c o", c=8),
          mkap(w_disp, 0, [[256, 128], [256 * 128, 8], [1, 256]]), eng="gpsimd")
    P.dma(Wv[:].rearrange("p (c o) -> p c o", c=8),
          mkap(w_val, 0, [[1024, 128], [1024 * 128, 8], [1, 1024]]), eng="gpsimd")
    P.dma(Wc[:].rearrange("p (c o) -> p c o", c=8),
          mkap(w_cproj, 0, [[1024, 128], [1024 * 128, 8], [1, 1024]]), eng="gpsimd")
    ld0, gp0 = P.cnt["SY"], P.cnt["GP"]

    P.wait("vector", "SY", ld0)
    P.wait("vector", "GP", gp0)
    P.op("vector", "memset", ytm[:], 0.0)
    P.op("vector", "memset", s_tiny[:], 0.0)
    P.op("vector", "memset", cm5[:], -5.0)
    P.op("vector", "tensor_scalar_mul", b0c[:], b0c[:], 0.5)
    P.op("vector", "memset", BDs[:], 0.0)
    P.op("vector", "memset", BDd[:], 0.0, inc="DV")
    bd_z = P.cnt["DV"]
    P.op("vector", "tensor_scalar_mul", wideP[:], sel01[:], wbc[:])
    P.op("vector", "tensor_scalar_mul", wideD[:], sel01[:], wdc[:], inc="DV")
    dv_wide = P.cnt["DV"]
    P.wait("gpsimd", "DV", bd_z)
    for blk in range(8):
        P.dma(BDs[16 * blk:16 * blk + 16, 16 * blk:16 * blk + 16], w_strain.ap(), eng="gpsimd", sem="BDL")
        P.dma(BDd[16 * blk:16 * blk + 16, 16 * blk:16 * blk + 16], w_dmg.ap(), eng="gpsimd", sem="BDL")
    gp_bd = P.cnt["BDL"]

    P.wait("tensor", "SY", ld0)
    P.op("tensor", "matmul", psA[0:64, 0:16], relT[:, 0:64], wpos_s[:],
         start=True, stop=True, inc="PE")
    P.wait("vector", "PE", P.cnt["PE"])
    P.op("vector", "tensor_copy", small[:], psA[0:64, 0:16], inc="DV")
    P.wait("sync", "DV", P.cnt["DV"])
    P.dma(mkap(r_dram, 0, [[16, 64], [1, 16]]), small[:])
    P.wait("sync", "SY")
    P.dma(rcol[:], mkap(r_dram, 0, [[16, 8], [1, 16], [128, 8]]),
          allow_slow_non_contiguous=True)
    P.wait("sync", "DV", dv_wide)
    for b in range(NWBUF):
        for k in range(4):
            P.dma(mkap(w_dram, (b * T + k * 512) * WROW, [[1024, 128], [1, 1024]]), ytm[:])
    for dr in (p_dram, q_dram):
        for k in range(4):
            P.dma(mkap(dr, k * 131072, [[1024, 128], [1, 1024]]), ytm[:])
        P.dma(mkap(dr, 524288, [[144, 128], [1, 144]]), ytm[:, 0:144])
    ld_const = P.cnt["SY"]

    def drain_all():
        for sn in ("SY", "LDX0", "LDX1", "WW0", "WW1", "PQD0", "PQD1", "SKW0", "SKW1", "STO", "BDL", "PT0", "PT1"):
            P.wait("sync", sn, P.cnt[sn])
        P.wait("gpsimd", "GP", P.cnt["GP"])

    def finish_early():
        P.wait("gpsimd", "SY", ld0)
        P.dma(mkap(y_out, 0, [[C, 128], [1, 128]]), sel01[:], eng="gpsimd")
        drain_all()
        P.emit()
        return nc

    if PHASE_LIM < 1:
        return finish_early()

    inv_c = 1.0 / C

    def ln_tile(buf, qt):
        P.op("vector", "tensor_reduce", c1[:], buf[:], AXX, ALU.add)
        P.op("vector", "tensor_scalar_mul", mcol[:, qt:qt + 1], c1[:], inv_c)
        P.op("vector", "scalar_tensor_tensor", ytm[:], buf[:], 0.0, buf[:],
             ALU.add, ALU.mult, accum_out=c1[:])
        P.op("vector", "tensor_scalar_mul", c1[:], c1[:], inv_c)
        P.op("vector", "tensor_tensor", c2[:], mcol[:, qt:qt + 1], mcol[:, qt:qt + 1], ALU.mult)
        P.op("vector", "scalar_tensor_tensor", c1[:], c1[:], 1e-5, c2[:],
             ALU.add, ALU.subtract, inc="DV")
        P.wait("scalar", "DV", P.cnt["DV"])
        P.op("scalar", "activation", c2[:], c1[:], AF.Sqrt, inc="AC")
        P.wait("vector", "AC", P.cnt["AC"])
        P.op("vector", "reciprocal", rstd[:, qt:qt + 1], c2[:])
        P.op("vector", "tensor_scalar", ytm[:], buf[:], mcol[:, qt:qt + 1],
             rstd[:, qt:qt + 1], ALU.subtract, ALU.mult, inc="DV")
        return P.cnt["DV"]

    xload_dv = {}
    ps_bf_free = 0
    ytm_pe = 0
    P.wait("vector", "SY", ld_const)
    for qt in range(NT):
        buf = xa if qt % 2 == 0 else xb
        if qt >= 2:
            P.wait("sync", "DV", xload_dv[qt - 2])
        lsl = "LDX" + str(qt % 2)
        P.dma(buf[:], mkap(x_in, qt * 128 * C, [[C, 128], [1, C]]), sem=lsl)
        P.wait("vector", lsl)
        P.wait("vector", "PE", ytm_pe)
        dv_n = ln_tile(buf, qt)
        xload_dv[qt] = dv_n
        P.wait("tensor", "DV", dv_n)
        P.wait("tensor", "DV", ps_bf_free)
        for c in range(8):
            P.op("tensor", "transpose", ps_bf[:, c * 128:c * 128 + 128],
                 ytm[:, c * 128:c * 128 + 128], ident[:],
                 inc=("PE" if c == 7 else None))
        ytm_pe = P.cnt["PE"]
        P.wait("vector", "PE", P.cnt["PE"])
        P.op("vector", "tensor_copy",
             mkap(big_fm, qt * 128, [[16384, 128], [2048, 8], [1, 128]]),
             ps_bf[:].rearrange("p (c t) -> p c t", c=8), inc="DV")
        ps_bf_free = P.cnt["DV"]

    yfm_dv = P.cnt["DV"]

    P.wait("tensor", "DV", yfm_dv)
    P.wait("tensor", "BDL", gp_bd)
    pqi = 0
    pq_done = {}
    di = 0
    dspb_free = [0, 0]
    for h in range(2):
        for tc in range(4):
            for c in range(8):
                P.op("tensor", "matmul", psA[:],
                     Wd[:, c * 256 + h * 128: c * 256 + h * 128 + 128],
                     mkap(big_fm, c * 2048 + tc * 512, [[16384, 128], [1, 512]]),
                     start=(c == 0), stop=(c == 7),
                     inc=("PE" if c == 7 else None))
            P.wait("vector", "PE", P.cnt["PE"])
            if dspb_free[di % 2]:
                P.wait("vector", "PE", dspb_free[di % 2])
            P.op("vector", "tensor_copy", dspb[di % 2][:], psA[:], inc="DV")
            P.wait("tensor", "DV", P.cnt["DV"])
            for (bd, dr, dbgk) in ((BDs, p_dram, "p_fm"), (BDd, q_dram, "q_fm")):
                P.wait("tensor", "DV", P.cnt["DV"])
                P.op("tensor", "matmul", psB[:], bd[:], dspb[di % 2][:],
                     start=True, stop=True, inc="PE")
                dspb_free[di % 2] = P.cnt["PE"]
                P.wait("vector", "PE", P.cnt["PE"])
                if pqi >= 2:
                    P.wait("vector", "PQD" + str(pqi % 2), pq_done[pqi - 2])
                P.op("vector", "tensor_copy", pq_buf[pqi % 2][:], psB[:], inc="DV")
                P.wait("sync", "DV", P.cnt["DV"])
                tok = P.dma(mkap(dr, h * 128 * PROW + 64 + tc * 512, [[PROW, 128], [1, 512]]),
                            pq_buf[pqi % 2][:], sem="PQD" + str(pqi % 2))
                if DEBUG:
                    tok = P.dma(mkap(dbg[dbgk], h * 2048 + tc * 512, [[4096, 128], [1, 512]]),
                                pq_buf[pqi % 2][:], sem="PQD" + str(pqi % 2))
                pq_done[pqi] = tok
                pqi += 1
    pq_w = (P.cnt["PQD0"], P.cnt["PQD1"])

    for qt in range(NT):
        P.wait("tensor", "DV", P.cnt["DV"])
        for half in range(2):
            ps = psA if half == 0 else psB
            for c in range(8):
                P.op("tensor", "matmul", ps[:],
                     mkap(big_fm, c * 2048 + qt * 128, [[16384, 128], [1, 128]]),
                     Wv[:, c * 1024 + half * 512: c * 1024 + half * 512 + 512],
                     start=(c == 0), stop=(c == 7),
                     inc=("PE" if c == 7 else None))
        P.wait("vector", "PE", P.cnt["PE"])
        P.op("vector", "tensor_copy", big_tm[:, qt * 1024: qt * 1024 + 512], psA[:])
        P.op("vector", "tensor_copy", big_tm[:, qt * 1024 + 512: qt * 1024 + 1024], psB[:],
             inc="DV")
    val_dv = P.cnt["DV"]

    if DEBUG:
        P.wait("sync", "DV", val_dv)
        P.dma(dbg["val"].ap(), big_tm[:])
        P.dma(dbg["yfm"].ap(), big_fm[:])

    for oc in range(8):
        P.wait("sync", "DV", P.cnt["DV"])
        lsl = "LDX" + str(oc % 2)
        wbuf = xa if oc % 2 == 0 else xb
        P.dma(wbuf[:], mkap(w_ffn, oc * 128 * C, [[C, 128], [1, C]]), sem=lsl)
        P.wait("vector", lsl)
        P.op("vector", "tensor_reduce", wamx[:, oc:oc + 1], wbuf[:], AXX, ALU.max,
             apply_absolute_value=True, inc="DV")
    P.op("vector", "tensor_reduce", c1[:], wamx[:], AXX, ALU.max,
         apply_absolute_value=True, inc="DV")
    P.wait("sync", "DV", P.cnt["DV"])
    P.dma(mkap(amax_dram, 0, [[1, 128], [0, 1]]), c1[:])
    P.wait("sync", "SY")
    P.dma(arow[:], mkap(amax_dram, 0, [[0, 1], [1, 128]]))
    P.wait("vector", "SY")
    P.op("vector", "tensor_reduce", s_tiny[0:1, 1:2], arow[:], AXX, ALU.max,
         apply_absolute_value=True)
    P.op("vector", "tensor_scalar_max", s_tiny[0:1, 1:2], s_tiny[0:1, 1:2], 1e-12)
    P.op("vector", "reciprocal", s_tiny[0:1, 3:4], s_tiny[0:1, 1:2])
    P.op("vector", "tensor_scalar_mul", s_tiny[0:1, 3:4], s_tiny[0:1, 3:4], 224.0, inc="DV")
    P.wait("sync", "DV", P.cnt["DV"])
    P.dma(mkap(sc_dram, 4, [[0, 1], [1, 1]]), s_tiny[0:1, 3:4])
    P.wait("sync", "SY")
    P.dma(scal[:, 1:2], mkap(sc_dram, 4, [[0, 128], [1, 1]]))
    sw2_tok = P.cnt["SY"]
    P.wait("vector", "SY", sw2_tok)
    wprep_dv = P.cnt["DV"]

    if PHASE_LIM < 2:
        return finish_early()

    state = {
        "zfree": [0, 0], "gfree": [0, 0], "wtj_free": [0, 0], "v_free": [0, 0],
        "sig_ac": 0, "comb_dv": 0, "wun_ta_pe": 0, "ps_bf_free": ps_bf_free,
        "skew_tok": [0] * NH, "wtil_dv": 0, "av_pe": [0] * NH,
        "attn_evac_dv": 0,
    }
    state["ps_bf_free"] = yfm_dv

    def av_chain(n):
        vslot = n % 2
        P.wait("tensor", "SY", state["skew_tok"][n])
        for qg in range(4):
            if not (n == 0 and qg == 0):
                P.wait("tensor", "DV", state["wtil_dv"])
            for k in range(4):
                qt = qg * 4 + k
                P.op("tensor", "transpose", ps_w[:, k * 192: k * 192 + 128],
                     Vself[vslot][:, qt * 128: qt * 128 + 128], ident[:])
                if qt > 0:
                    P.op("tensor", "transpose", ps_w[:, k * 192 + 128: k * 192 + 192],
                         Vcorn[vslot][:, qt * 128: qt * 128 + 128], ident[0:64, 0:64])
            P.op("tensor", "engine_nop", inc="PE") if False else None
            P.op("tensor", "transpose", ps_w[:, (3) * 192 + 128 + 0: (3) * 192 + 192]
                 if qg * 4 + 3 > 0 else ps_w[:, 0:128],
                 Vcorn[vslot][:, (qg * 4 + 3) * 128: (qg * 4 + 3) * 128 + 128]
                 if qg * 4 + 3 > 0 else Vself[vslot][:, 0:128],
                 ident[0:64, 0:64] if qg * 4 + 3 > 0 else ident[:], inc="PE")
            tr_pe = P.cnt["PE"]
            P.wait("vector", "PE", tr_pe)
            P.op("vector", "tensor_copy", Wtil[:, qg * 768: qg * 768 + 768],
                 ps_w[:, 0:768], inc="DV")
            state["wtil_dv"] = P.cnt["DV"]
            P.wait("tensor", "DV", state["wtil_dv"])
            if n >= 2:
                P.wait("tensor", "DV", state["attn_evac_dv"])
            for k in range(4):
                qt = qg * 4 + k
                P.op("tensor", "matmul", psA[0:64, k * 128: k * 128 + 128],
                     big_tm[:, qt * 1024 + 64 * n: qt * 1024 + 64 * n + 64],
                     Wtil[:, qg * 768 + k * 192: qg * 768 + k * 192 + 128],
                     start=True, stop=(qt == 0))
                if qt > 0:
                    P.op("tensor", "matmul", psA[0:64, k * 128: k * 128 + 64],
                         big_tm[:, (qt - 1) * 1024 + 64 * n: (qt - 1) * 1024 + 64 * n + 64],
                         Wtil[:, qg * 768 + k * 192 + 128: qg * 768 + k * 192 + 192],
                         start=False, stop=True)
            P.op("tensor", "transpose", ps_w[:, 0:0 + 128] if False else psA[0:1, 0:1],
                 None, None) if False else None
            pass
        return

    def av_chain(n):
        vslot = n % 2
        P.wait("tensor", "SKW" + str(vslot), state["skew_tok"][n])
        for qg in range(4):
            if not (n == 0 and qg == 0):
                P.wait("tensor", "DV", state["wtil_dv"])
            ops = []
            for k in range(4):
                qt = qg * 4 + k
                ops.append(("transpose", ps_w[:, k * 192: k * 192 + 128],
                            Vself[vslot][:, qt * 128: qt * 128 + 128], ident[:]))
                ops.append(("transpose", ps_w[:, k * 192 + 128: k * 192 + 192],
                            Vcorn[vslot][:, qt * 128: qt * 128 + 128],
                            ident[64 * vslot: 64 * vslot + 64, 64 * vslot: 64 * vslot + 64]))
            for j, o in enumerate(ops):
                P.op("tensor", o[0], *o[1:], inc=("PE" if j == len(ops) - 1 else None))
            P.wait("vector", "PE", P.cnt["PE"])
            P.op("vector", "tensor_copy", Wtil[:, qg * 768: qg * 768 + 768],
                 ps_w[:, 0:768], inc="DV")
            state["wtil_dv"] = P.cnt["DV"]
            P.wait("tensor", "DV", state["wtil_dv"])
            if state["attn_evac_dv"]:
                P.wait("tensor", "DV", state["attn_evac_dv"])
            mms = []
            for k in range(4):
                qt = qg * 4 + k
                mms.append((psA[0:64, k * 128: k * 128 + 128],
                            big_tm[:, qt * 1024 + 64 * n: qt * 1024 + 64 * n + 64],
                            Wtil[:, qg * 768 + k * 192: qg * 768 + k * 192 + 128],
                            True, qt == 0))
                if qt > 0:
                    mms.append((psA[0:64, k * 128: k * 128 + 64],
                                big_tm[:, (qt - 1) * 1024 + 64 * n: (qt - 1) * 1024 + 64 * n + 64],
                                Wtil[:, qg * 768 + k * 192 + 128: qg * 768 + k * 192 + 192],
                                False, True))
            for j, m in enumerate(mms):
                P.op("tensor", "matmul", m[0], m[1], m[2], start=m[3], stop=m[4],
                     inc=("PE" if j == len(mms) - 1 else None))
            av_pe = P.cnt["PE"]
            P.wait("vector", "PE", av_pe)
            r0 = 64 * (n % 2)
            P.op("vector", "tensor_copy",
                 big_fm[r0:r0 + 64, (n // 2) * 2048 + qg * 512: (n // 2) * 2048 + qg * 512 + 512],
                 psA[0:64, :], inc="DV")
            state["attn_evac_dv"] = P.cnt["DV"]
        state["v_free"][vslot] = P.cnt["PE"]

    def pt_load(hp, X):
        ptsem = "PT" + str((hp * 2 + X) % 2)
        rep, bc, dr = (repQ, bcQ, q_dram) if X == 0 else (repP, bcP, p_dram)
        for (i, n) in ((0, 2 * hp), (1, 2 * hp + 1)):
            P.dma(rep[i][:], mkap(dr, (16 * n) * PROW, [[1, 8], [PROW, 16], [1, 2112]]),
                  eng="gpsimd", sem=ptsem)
            P.dma(bc[i][:], mkap(dr, (16 * n) * PROW + 64, [[0, 8], [PROW, 16], [1, T]]),
                  eng="gpsimd", sem=ptsem)
        return ptsem, P.cnt[ptsem]

    P.wait("gpsimd", "PQD0", pq_w[0])
    P.wait("gpsimd", "PQD1", pq_w[1])
    P.wait("gpsimd", "SY", ld_const)
    pt_tok = {(0, 0): pt_load(0, 0), (0, 1): pt_load(0, 1)}

    for hp in range(8):
        for X in range(2):
            rep, bc = (repQ, bcQ) if X == 0 else (repP, bcP)
            wide = wideD if X == 0 else wideP
            ptsem, my_pt = pt_tok[(hp, X)]
            for sub in range(2):
                for g in range(8):
                    i = (X * 16 + sub * 8 + g) % 2
                    P.wait("vector", ptsem, my_pt)
                    P.wait("vector", "AC", state["zfree"][i])
                    P.op("vector", "tensor_tensor", zsl[i][:],
                         rep[sub][:, 1 + 8 * g: 1 + 8 * g + T], bc[sub][:],
                         ALU.subtract, inc="DV")
                    P.wait("scalar", "DV", P.cnt["DV"])
                    P.wait("scalar", "PE", state["gfree"][i])
                    bias = rcol[:, g:g + 1] if X == 1 else bdmgc[:]
                    state["zfree"][i] = P.op("scalar", "activation", gsl[i][:], zsl[i][:],
                                             AF.Gelu, bias=bias, inc="AC")
                    P.wait("tensor", "AC", state["zfree"][i])
                    if sub == 0 and g == 0:
                        if X == 0 and hp > 0:
                            P.wait("tensor", "DV", state["comb_dv"])
                        if X == 1:
                            P.wait("tensor", "AC", state["sig_ac"])
                    for ch in range(4):
                        P.op("tensor", "matmul",
                             ps_ctr[64 * sub:64 * sub + 64, ch * 512: ch * 512 + 512],
                             wide[:, 56 - 8 * g: 56 - 8 * g + 64],
                             gsl[i][:, ch * 512: ch * 512 + 512],
                             start=(g == 0), stop=(g == 7),
                             inc=("PE" if ch == 3 else None))
                    state["gfree"][i] = P.cnt["PE"]
            if hp < 7:
                P.wait("gpsimd", "DV", P.cnt["DV"])
                pt_tok[(hp + 1, X)] = pt_load(hp + 1, X)
            if X == 0:
                P.wait("scalar", "PE", P.cnt["PE"])
                if hp > 0:
                    P.wait("scalar", "DV", state["comb_dv"])
                state["sig_ac"] = P.op("scalar", "activation", sgm[:], ps_ctr[:],
                                       AF.Tanh, bias=b0c[:], scale=0.5, inc="AC")

        P.wait("vector", "PE", P.cnt["PE"])
        P.wait("vector", "AC", state["sig_ac"])
        P.op("vector", "scalar_tensor_tensor", sgm[:], sgm[:], -5.0, ps_ctr[:],
             ALU.mult, ALU.add, inc="DV")
        state["comb_dv"] = P.cnt["DV"]
        P.wait("scalar", "DV", state["comb_dv"])
        P.wait("scalar", "PE", state["wun_ta_pe"])
        if DEBUG and state.get("dbg_sy"):
            P.wait("scalar", "SY", state["dbg_sy"])
        e_ac = P.op("scalar", "activation", w_un[:], sgm[:], AF.Exp,
                    bias=cm5[:], inc="AC")
        P.wait("vector", "AC", e_ac)
        P.op("vector", "tensor_tensor", w_un[:, 0:64], w_un[:, 0:64], mask01[:],
             ALU.mult, inc="DV")
        pair_mask_dv = P.cnt["DV"]

        for sub in range(2):
            n = 2 * hp + sub
            wslot = n % 2
            P.wait("tensor", "DV", pair_mask_dv)
            P.wait("tensor", "DV", state["ps_bf_free"])
            for qt in range(NT):
                P.op("tensor", "transpose", ps_bf[:, qt * 64: qt * 64 + 64],
                     w_un[64 * sub: 64 * sub + 64, qt * 128: qt * 128 + 128],
                     ident[64 * sub: 64 * sub + 64, 64 * sub: 64 * sub + 64],
                     inc=("PE" if qt == NT - 1 else None))
            ta_pe = P.cnt["PE"]
            if sub == 1:
                state["wun_ta_pe"] = ta_pe
            P.wait("vector", "PE", ta_pe)
            P.wait("vector", "WW" + str(wslot), state["wtj_free"][wslot])
            P.op("vector", "tensor_copy", w_tj[wslot][:], ps_bf[:], inc="DV")
            state["ps_bf_free"] = P.cnt["DV"]
            P.op("vector", "tensor_reduce", wsum[:],
                 w_tj[wslot][:].rearrange("p (a j) -> p a j", a=16), AXX, ALU.add)
            P.op("vector", "reciprocal", wrcp[:], wsum[:])
            P.op("vector", "tensor_tensor",
                 w_tj[wslot][:].rearrange("p (a j) -> p a j", a=16),
                 w_tj[wslot][:].rearrange("p (a j) -> p a j", a=16),
                 mkap(wrcp, 0, [[16, 128], [1, 16], [0, 64]]),
                 ALU.mult, inc="DV")
            norm_dv = P.cnt["DV"]
            if DEBUG and n == 0:
                P.wait("sync", "DV", pair_mask_dv)
                P.dma(dbg["wun"].ap(), w_un[:])
                P.wait("sync", "DV", norm_dv)
                P.dma(dbg["wtj"].ap(), w_tj[wslot][:])
                state["dbg_sy"] = P.cnt["SY"]
            P.wait("sync", "DV", norm_dv)
            wb = (n % NWBUF) * T * WROW
            state["wtj_free"][wslot] = P.dma(
                mkap(w_dram, wb + 128, [[WROW, 128], [128 * WROW, 16], [1, 64]]),
                w_tj[wslot][:].rearrange("p (a j) -> p a j", a=16),
                sem="WW" + str(wslot))
            if n >= 2:
                av_chain(n - 2)
            vslot = n % 2
            P.wait("sync", "PE", state["v_free"][vslot])
            P.wait("sync", "WW" + str(wslot), state["wtj_free"][wslot])
            P.dma(Vself[vslot][:],
                  mkap(w_dram, wb + 191, [[255, 128], [32768, 16], [1, 128]]),
                  sem="SKW" + str(vslot))
            state["skew_tok"][n] = P.dma(
                Vcorn[vslot][:], mkap(w_dram, wb + 63, [[255, 64], [32768, 16], [1, 128]]),
                sem="SKW" + str(vslot))

    av_chain(NH - 2)
    av_chain(NH - 1)
    attn_dv = P.cnt["DV"]

    if DEBUG:
        P.wait("sync", "DV", attn_dv)
        P.dma(dbg["attn"].ap(), big_fm[:])

    if PHASE_LIM < 4:
        return finish_early()

    x2_w = {}
    for qt in range(NT):
        buf = xa if qt % 2 == 0 else xb
        if qt >= 2:
            P.wait("sync", "DV", x2_w[qt - 2][0])
        else:
            P.wait("sync", "DV", wprep_dv)
        lsl = "LDX" + str(qt % 2)
        P.dma(buf[:], mkap(x_in, qt * 128 * C, [[C, 128], [1, C]]), sem=lsl)
        ld_tok = (lsl, P.cnt[lsl])
        if qt >= 2:
            P.wait("tensor", "DV", x2_w[qt - 2][0])
        P.wait("tensor", "DV", attn_dv)
        pA = psA[:] if qt % 2 == 0 else ps_ctr[:, 0:512]
        pB = psB[:] if qt % 2 == 0 else ps_ctr[:, 512:1024]
        for half in range(2):
            ps = pA if half == 0 else pB
            for c in range(8):
                P.op("tensor", "matmul", ps,
                     big_fm[:, c * 2048 + qt * 128: c * 2048 + qt * 128 + 128],
                     Wc[:, c * 1024 + half * 512: c * 1024 + half * 512 + 512],
                     start=(c == 0), stop=(c == 7),
                     inc=("PE" if c == 7 else None))
        P.wait("vector", "PE", P.cnt["PE"])
        P.wait("vector", ld_tok[0], ld_tok[1])
        if qt >= 1:
            P.wait("vector", "STO", x2_w[qt - 1][1])
        P.op("vector", "scalar_tensor_tensor", y2scr[:, 0:512], pA, 0.0,
             buf[:, 0:512], ALU.add, ALU.add)
        P.op("vector", "scalar_tensor_tensor", y2scr[:, 512:1024], pB, 0.0,
             buf[:, 512:1024], ALU.add, ALU.add, inc="DV")
        dv_tok = P.cnt["DV"]
        P.wait("sync", "DV", dv_tok)
        tok = P.dma(mkap(x2_dram, qt * 128 * C, [[C, 128], [1, C]]), y2scr[:], sem="STO")
        P.op("vector", "tensor_reduce", c1[:], y2scr[:], AXX, ALU.add)
        P.op("vector", "tensor_scalar_mul", mcol[:, qt:qt + 1], c1[:], inv_c)
        P.op("vector", "scalar_tensor_tensor", ytm[:], y2scr[:], 0.0, y2scr[:],
             ALU.add, ALU.mult, accum_out=c1[:])
        P.op("vector", "tensor_scalar_mul", c1[:], c1[:], inv_c)
        P.op("vector", "tensor_tensor", c2[:], mcol[:, qt:qt + 1], mcol[:, qt:qt + 1], ALU.mult)
        P.op("vector", "scalar_tensor_tensor", c1[:], c1[:], 1e-5, c2[:],
             ALU.add, ALU.subtract, inc="DV")
        P.wait("scalar", "DV", P.cnt["DV"])
        P.op("scalar", "activation", c2[:], c1[:], AF.Sqrt, inc="AC")
        P.wait("vector", "AC", P.cnt["AC"])
        P.op("vector", "reciprocal", rstd[:, qt:qt + 1], c2[:])
        P.op("vector", "tensor_reduce", c1[:], y2scr[:], AXX, ALU.max)
        P.op("vector", "tensor_tensor", c1[:], c1[:], mcol[:, qt:qt + 1], ALU.subtract)
        P.op("vector", "tensor_reduce", c2[:], y2scr[:], AXX, ALU.min)
        P.op("vector", "tensor_tensor", c2[:], mcol[:, qt:qt + 1], c2[:], ALU.subtract)
        P.op("vector", "tensor_tensor", c1[:], c1[:], c2[:], ALU.max)
        P.op("vector", "tensor_tensor", amx[:, qt:qt + 1], c1[:], rstd[:, qt:qt + 1],
             ALU.mult, inc="DV")
        x2_w[qt] = (P.cnt["DV"], tok)
    x2_done = P.cnt["STO"]

    if DEBUG or os.environ.get("KDEBUG2") == "1":
        P.wait("sync", "STO", x2_done)
        dbg_x2 = dbg.get("x2d") or nc.dram_tensor("dbg_x2d", [T * C], F32, kind="ExternalOutput")
        P.dma(dbg_x2.ap(), x2_dram.ap())

    if PHASE_LIM < 5:
        return finish_early()


    P.op("vector", "tensor_reduce", c1[:], amx[:], AXX, ALU.max,
         apply_absolute_value=True, inc="DV")
    P.wait("sync", "DV", P.cnt["DV"])
    P.dma(mkap(amax_dram, 0, [[1, 128], [0, 1]]), c1[:])
    P.wait("sync", "SY")
    P.dma(arow[:], mkap(amax_dram, 0, [[0, 1], [1, 128]]))
    P.wait("vector", "SY")
    P.op("vector", "tensor_reduce", s_tiny[0:1, 0:1], arow[:], AXX, ALU.max,
         apply_absolute_value=True, inc="DV")
    xam_dv = P.cnt["DV"]

    P.wait("gpsimd", "DV", xam_dv)
    P.dma(cc_in.ap(), arow[:], eng="gpsimd")
    P.wait("gpsimd", "GP")
    cc_tok = P.collective("AllReduce", ALU.max, [[0, 1, 2, 3, 4, 5, 6, 7]],
                          [cc_in.ap()], [cc_out.ap()])
    P.wait("gpsimd", "CC", cc_tok)
    P.dma(arow[:], cc_out.ap(), eng="gpsimd")
    gp_amax = P.cnt["GP"]

    ps_bf_tok = 0
    for oc in range(8):
        buf = xa if oc % 2 == 0 else xb
        P.wait("sync", "DV", P.cnt["DV"])
        lsl = "LDX" + str(oc % 2)
        P.dma(buf[:], mkap(w_ffn, oc * 128 * C, [[C, 128], [1, C]]), sem=lsl)
        P.wait("vector", lsl)
        w8v = (xb if oc % 2 == 0 else xa).bitcast(FP8)
        P.wait("vector", "PE", P.cnt["PE"])
        P.op("vector", "tensor_scalar_mul", w8v[:, 0:1024], buf[:], scal[:, 1:2])
        P.op("vector", "tensor_copy", ytm[:], w8v[:, 0:1024], inc="DV")
        P.wait("tensor", "DV", P.cnt["DV"])
        if ps_bf_tok:
            P.wait("tensor", "DV", ps_bf_tok)
        for dc in range(8):
            P.op("tensor", "transpose", ps_bf[:, dc * 128: dc * 128 + 128],
                 ytm[:, dc * 128: dc * 128 + 128], ident[:],
                 inc=("PE" if dc == 7 else None))
        P.wait("vector", "PE", P.cnt["PE"])
        P.op("vector", "tensor_copy",
             mkap(big_tm, oc * 128, [[16384, 128], [1024, 8], [1, 128]]),
             ps_bf[:].rearrange("p (dc o) -> p dc o", dc=8), inc="DV")
        ps_bf_tok = P.cnt["DV"]
    wffnt_dv = P.cnt["DV"]

    P.wait("vector", "GP", gp_amax)
    P.op("vector", "tensor_reduce", s_tiny[0:1, 0:1], arow[:], AXX, ALU.max,
         apply_absolute_value=True)
    P.op("vector", "tensor_scalar_max", s_tiny[0:1, 0:1], s_tiny[0:1, 0:1], 1e-12)
    P.op("vector", "reciprocal", s_tiny[0:1, 2:3], s_tiny[0:1, 0:1])
    P.op("vector", "tensor_scalar_mul", s_tiny[0:1, 2:3], s_tiny[0:1, 2:3], 224.0)
    P.op("vector", "tensor_tensor", s_tiny[0:1, 4:5], s_tiny[0:1, 0:1],
         s_tiny[0:1, 1:2], ALU.mult)
    P.op("vector", "reciprocal", s_tiny[0:1, 5:6], s_tiny[0:1, 4:5])
    P.op("vector", "tensor_scalar_mul", s_tiny[0:1, 4:5], s_tiny[0:1, 4:5], 1.0 / 50176.0)
    P.op("vector", "tensor_scalar_mul", s_tiny[0:1, 5:6], s_tiny[0:1, 5:6], 50176.0, inc="DV")
    P.wait("sync", "DV", P.cnt["DV"])
    P.dma(mkap(sc_dram, 0, [[0, 1], [1, 8]]), s_tiny[:])
    P.wait("sync", "SY")
    P.dma(scal[:, 0:1], mkap(sc_dram, 2, [[0, 128], [1, 1]]))
    P.dma(scal[:, 2:4], mkap(sc_dram, 4 + 0, [[0, 128], [1, 2]]))
    scal_tok = P.cnt["SY"]

    P.wait("vector", "SY", scal_tok)
    P.op("vector", "tensor_scalar_mul", brow_bf[:], brow[:], s_tiny[0:1, 5:6], inc="DV")


    if DEBUG:
        P.wait("sync", "DV", wffnt_dv)
        P.dma(dbg["wffnt"].ap(), big_tm[:, 0:8192])

    brow_dv = P.cnt["DV"]

    if PHASE_LIM < 6:
        return finish_early()

    xq8v = sgm.bitcast(FP8)
    outS = [zsl[0].bitcast(F32), zsl[1].bitcast(F32)]
    q6dv = {}
    bf_cp = {}
    mm_pe = {}
    st_tok = {}
    pe_wprep = P.cnt["PE"]
    P.wait("vector", "STO", x2_done)
    P.wait("vector", "PE", pe_wprep)

    def evac_store(qt):
        buf = xa if qt % 2 == 0 else xb
        pA = psA[:] if qt % 2 == 0 else ps_ctr[:, 0:512]
        pB = psB[:] if qt % 2 == 0 else ps_ctr[:, 512:1024]
        ob = outS[qt % 2]
        ssem = "STO" if qt % 2 == 0 else "PT0"
        P.wait("vector", "PE", mm_pe[qt])
        if qt >= 2:
            P.wait("vector", ssem, st_tok[qt - 2])
        P.op("vector", "scalar_tensor_tensor", ob[:, 0:512], pA, scal[:, 2:3],
             buf[:, 0:512], ALU.mult, ALU.add)
        P.op("vector", "scalar_tensor_tensor", ob[:, 512:1024], pB, scal[:, 2:3],
             buf[:, 512:1024], ALU.mult, ALU.add, inc="DV")
        q6dv[qt] = P.cnt["DV"]
        dst = mkap(y_out, qt * 128 * C, [[C, 128], [1, C]])
        if qt % 2 == 0:
            P.wait("sync", "DV", q6dv[qt])
            st_tok[qt] = P.dma(dst, ob[:], sem="STO")
        else:
            P.wait("gpsimd", "DV", q6dv[qt])
            st_tok[qt] = P.dma(dst, ob[:], sem="PT0", eng="gpsimd")

    for qt in range(NT):
        buf = xa if qt % 2 == 0 else xb
        if qt >= 2:
            P.wait("sync", "DV", q6dv[qt - 2])
        else:
            P.wait("sync", "DV", wffnt_dv)
        lsl = "LDX" + str(qt % 2)
        P.dma(buf[:], mkap(x2_dram, qt * 128 * C, [[C, 128], [1, C]]), sem=lsl)
        P.wait("vector", lsl)
        P.op("vector", "tensor_scalar", y2scr[:], buf[:], mcol[:, qt:qt + 1],
             rstd[:, qt:qt + 1], ALU.subtract, ALU.mult)
        P.op("vector", "tensor_scalar_mul", xq8v[:, 0:1024], y2scr[:], scal[:, 0:1])
        P.op("vector", "tensor_copy", ytm[:], xq8v[:, 0:1024], inc="DV")
        P.wait("tensor", "DV", P.cnt["DV"])
        P.wait("tensor", "DV", bf_cp[qt - 1] if qt >= 1 else wffnt_dv)
        for dc in range(8):
            P.op("tensor", "transpose", ps_bf[:, dc * 128: dc * 128 + 128],
                 ytm[:, dc * 128: dc * 128 + 128], ident[:],
                 inc=("PE" if dc == 7 else None))
        P.wait("vector", "PE", P.cnt["PE"])
        P.op("vector", "tensor_copy",
             mkap(big_fm, qt * 128, [[16384, 128], [2048, 8], [1, 128]]),
             ps_bf[:].rearrange("p (dc t) -> p dc t", dc=8), inc="DV")
        bf_cp[qt] = P.cnt["DV"]
        P.wait("tensor", "DV", bf_cp[qt])
        P.wait("tensor", "DV", brow_dv)
        if qt >= 2:
            P.wait("tensor", "DV", q6dv[qt - 2])
        pA = psA[:] if qt % 2 == 0 else ps_ctr[:, 0:512]
        pB = psB[:] if qt % 2 == 0 else ps_ctr[:, 512:1024]
        for half in range(2):
            ps = pA if half == 0 else pB
            for dc in range(8):
                P.op("tensor", "matmul", ps,
                     big_fm[:, dc * 2048 + qt * 128: dc * 2048 + qt * 128 + 128],
                     big_tm[:, dc * 1024 + half * 512: dc * 1024 + half * 512 + 512],
                     start=(dc == 0), stop=False)
            P.op("tensor", "matmul", ps, onesr[:],
                 brow_bf[0:1, half * 512: half * 512 + 512],
                 start=False, stop=True, inc="PE")
        mm_pe[qt] = P.cnt["PE"]
        if qt >= 1:
            evac_store(qt - 1)
    evac_store(NT - 1)

    if DEBUG or os.environ.get("KDEBUG2") == "1":
        P.wait("sync", "STO", x2_done)
        dbg_x2 = dbg.get("x2d") or nc.dram_tensor("dbg_x2d", [T * C], F32, kind="ExternalOutput")
        P.dma(dbg_x2.ap(), x2_dram.ap())

    if PHASE_LIM < 5:
        return finish_early()


    P.op("vector", "tensor_reduce", c1[:], amx[:], AXX, ALU.max,
         apply_absolute_value=True, inc="DV")
    P.wait("sync", "DV", P.cnt["DV"])
    P.dma(mkap(amax_dram, 0, [[1, 128], [0, 1]]), c1[:])
    P.wait("sync", "SY")
    P.dma(arow[:], mkap(amax_dram, 0, [[0, 1], [1, 128]]))
    P.wait("vector", "SY")
    P.op("vector", "tensor_reduce", s_tiny[0:1, 0:1], arow[:], AXX, ALU.max,
         apply_absolute_value=True, inc="DV")
    xam_dv = P.cnt["DV"]

    P.wait("gpsimd", "DV", xam_dv)
    P.dma(cc_in.ap(), arow[:], eng="gpsimd")
    P.wait("gpsimd", "GP")
    cc_tok = P.collective("AllReduce", ALU.max, [[0, 1, 2, 3, 4, 5, 6, 7]],
                          [cc_in.ap()], [cc_out.ap()])
    P.wait("gpsimd", "CC", cc_tok)
    P.dma(arow[:], cc_out.ap(), eng="gpsimd")
    gp_amax = P.cnt["GP"]

    ps_bf_tok = 0
    for oc in range(8):
        buf = xa if oc % 2 == 0 else xb
        P.wait("sync", "DV", P.cnt["DV"])
        lsl = "LDX" + str(oc % 2)
        P.dma(buf[:], mkap(w_ffn, oc * 128 * C, [[C, 128], [1, C]]), sem=lsl)
        P.wait("vector", lsl)
        w8v = (xb if oc % 2 == 0 else xa).bitcast(FP8)
        P.wait("vector", "PE", P.cnt["PE"])
        P.op("vector", "tensor_scalar_mul", w8v[:, 0:1024], buf[:], scal[:, 1:2])
        P.op("vector", "tensor_copy", ytm[:], w8v[:, 0:1024], inc="DV")
        P.wait("tensor", "DV", P.cnt["DV"])
        if ps_bf_tok:
            P.wait("tensor", "DV", ps_bf_tok)
        for dc in range(8):
            P.op("tensor", "transpose", ps_bf[:, dc * 128: dc * 128 + 128],
                 ytm[:, dc * 128: dc * 128 + 128], ident[:],
                 inc=("PE" if dc == 7 else None))
        P.wait("vector", "PE", P.cnt["PE"])
        P.op("vector", "tensor_copy",
             mkap(big_tm, oc * 128, [[16384, 128], [1024, 8], [1, 128]]),
             ps_bf[:].rearrange("p (dc o) -> p dc o", dc=8), inc="DV")
        ps_bf_tok = P.cnt["DV"]
    wffnt_dv = P.cnt["DV"]

    P.wait("vector", "GP", gp_amax)
    P.op("vector", "tensor_reduce", s_tiny[0:1, 0:1], arow[:], AXX, ALU.max,
         apply_absolute_value=True)
    P.op("vector", "tensor_scalar_max", s_tiny[0:1, 0:1], s_tiny[0:1, 0:1], 1e-12)
    P.op("vector", "reciprocal", s_tiny[0:1, 2:3], s_tiny[0:1, 0:1])
    P.op("vector", "tensor_scalar_mul", s_tiny[0:1, 2:3], s_tiny[0:1, 2:3], 224.0)
    P.op("vector", "tensor_tensor", s_tiny[0:1, 4:5], s_tiny[0:1, 0:1],
         s_tiny[0:1, 1:2], ALU.mult)
    P.op("vector", "reciprocal", s_tiny[0:1, 5:6], s_tiny[0:1, 4:5])
    P.op("vector", "tensor_scalar_mul", s_tiny[0:1, 4:5], s_tiny[0:1, 4:5], 1.0 / 50176.0)
    P.op("vector", "tensor_scalar_mul", s_tiny[0:1, 5:6], s_tiny[0:1, 5:6], 50176.0, inc="DV")
    P.wait("sync", "DV", P.cnt["DV"])
    P.dma(mkap(sc_dram, 0, [[0, 1], [1, 8]]), s_tiny[:])
    P.wait("sync", "SY")
    P.dma(scal[:, 0:1], mkap(sc_dram, 2, [[0, 128], [1, 1]]))
    P.dma(scal[:, 2:4], mkap(sc_dram, 4 + 0, [[0, 128], [1, 2]]))
    scal_tok = P.cnt["SY"]

    P.wait("vector", "SY", scal_tok)
    P.op("vector", "tensor_scalar_mul", brow_bf[:], brow[:], s_tiny[0:1, 5:6], inc="DV")


    if DEBUG:
        P.wait("sync", "DV", wffnt_dv)
        P.dma(dbg["wffnt"].ap(), big_tm[:, 0:8192])

    brow_dv = P.cnt["DV"]

    if PHASE_LIM < 6:
        return finish_early()

    xq8v = sgm.bitcast(FP8)
    outS = [zsl[0].bitcast(F32), zsl[1].bitcast(F32)]
    out_toks = []
    q6dv = {}
    bf_cp = {}
    pe_wprep = P.cnt["PE"]
    P.wait("vector", "STO", x2_done)
    P.wait("vector", "PE", pe_wprep)
    for qt in range(NT):
        buf = xa if qt % 2 == 0 else xb
        if qt >= 2:
            P.wait("sync", "DV", q6dv[qt - 2])
        else:
            P.wait("sync", "DV", wffnt_dv)
        lsl = "LDX" + str(qt % 2)
        P.dma(buf[:], mkap(x2_dram, qt * 128 * C, [[C, 128], [1, C]]), sem=lsl)
        P.wait("vector", lsl)
        P.op("vector", "tensor_scalar", y2scr[:], buf[:], mcol[:, qt:qt + 1],
             rstd[:, qt:qt + 1], ALU.subtract, ALU.mult)
        P.op("vector", "tensor_scalar_mul", xq8v[:, 0:1024], y2scr[:], scal[:, 0:1])
        P.op("vector", "tensor_copy", ytm[:], xq8v[:, 0:1024], inc="DV")
        P.wait("tensor", "DV", P.cnt["DV"])
        P.wait("tensor", "DV", bf_cp[qt - 1] if qt >= 1 else wffnt_dv)
        for dc in range(8):
            P.op("tensor", "transpose", ps_bf[:, dc * 128: dc * 128 + 128],
                 ytm[:, dc * 128: dc * 128 + 128], ident[:],
                 inc=("PE" if dc == 7 else None))
        P.wait("vector", "PE", P.cnt["PE"])
        P.op("vector", "tensor_copy",
             mkap(big_fm, qt * 128, [[16384, 128], [2048, 8], [1, 128]]),
             ps_bf[:].rearrange("p (dc t) -> p dc t", dc=8), inc="DV")
        bf_cp[qt] = P.cnt["DV"]
        P.wait("tensor", "DV", bf_cp[qt])
        P.wait("tensor", "DV", brow_dv)
        if qt >= 2:
            P.wait("tensor", "DV", q6dv[qt - 2])
        pA = psA[:] if qt % 2 == 0 else ps_ctr[:, 0:512]
        pB = psB[:] if qt % 2 == 0 else ps_ctr[:, 512:1024]
        for half in range(2):
            ps = pA if half == 0 else pB
            for dc in range(8):
                P.op("tensor", "matmul", ps,
                     big_fm[:, dc * 2048 + qt * 128: dc * 2048 + qt * 128 + 128],
                     big_tm[:, dc * 1024 + half * 512: dc * 1024 + half * 512 + 512],
                     start=(dc == 0), stop=False)
            P.op("tensor", "matmul", ps, onesr[:],
                 brow_bf[0:1, half * 512: half * 512 + 512],
                 start=False, stop=True, inc="PE")
        P.wait("vector", "PE", P.cnt["PE"])
        ob = outS[qt % 2]
        if len(out_toks) >= 2:
            P.wait("vector", "STO", out_toks[-2])
        P.op("vector", "scalar_tensor_tensor", ob[:, 0:512], pA, scal[:, 2:3],
             buf[:, 0:512], ALU.mult, ALU.add)
        P.op("vector", "scalar_tensor_tensor", ob[:, 512:1024], pB, scal[:, 2:3],
             buf[:, 512:1024], ALU.mult, ALU.add, inc="DV")
        q6dv[qt] = P.cnt["DV"]
        P.wait("sync", "DV", q6dv[qt])
        out_toks.append(P.dma(mkap(y_out, qt * 128 * C, [[C, 128], [1, C]]), ob[:], sem="STO"))

    if DEBUG or os.environ.get("KDEBUG2") == "1":
        dbg_xq = dbg.get("xq") or nc.dram_tensor("dbg_xq", [128, 16384], BF16, kind="ExternalOutput")
        dbg_wf = dbg.get("wffnt") or nc.dram_tensor("dbg_wffnt", [128, 8192], BF16, kind="ExternalOutput")
        P.wait("sync", "DV", P.cnt["DV"])
        P.dma(dbg_xq.ap(), big_fm[:])
        P.dma(dbg_wf.ap(), big_tm[:, 0:8192])

    drain_all()

    P.emit()
    return nc


def host_constants():
    ident = np.eye(128, dtype=ml_dtypes.bfloat16)
    sel01 = np.zeros((128, 128), dtype=ml_dtypes.bfloat16)
    for p in range(128):
        sel01[p, 56 + p // 16] = 1.0
    mask01 = np.ones((128, 64), dtype=ml_dtypes.bfloat16)
    for jj in range(64):
        for t in range(64):
            if jj < 63 - t:
                mask01[jj, t] = 0.0
                mask01[jj + 64, t] = 0.0
    ones = np.ones((1, 128), dtype=ml_dtypes.bfloat16)
    return ident, sel01, mask01, ones


_NC_CACHE = {}


def kernel(L0_attn_x, L0_attn_W_disp, L0_attn_W_val, L0_attn_rel_pos_emb,
           L0_attn_W_strain, L0_attn_W_pos, L0_attn_W_bond, L0_attn_W_dmg,
           L0_attn_b_dmg, L0_attn_W_dmg_out, L0_attn_b_dmg_out, L0_attn_W_cproj,
           L0_ffn_weight, L0_ffn_bias):
    if "nc" not in _NC_CACHE:
        _NC_CACHE["nc"] = build_nc()
    nc = _NC_CACHE["nc"]
    ident, sel01, mask01, ones = host_constants()

    def h(v):
        a = np.asarray(v)
        return np.ascontiguousarray(a.astype(np.float32, copy=False))

    x = h(L0_attn_x)
    common = {
        "w_disp": h(L0_attn_W_disp),
        "w_val": h(L0_attn_W_val),
        "rel_pos": h(L0_attn_rel_pos_emb),
        "w_strain": h(L0_attn_W_strain),
        "w_pos": h(L0_attn_W_pos),
        "w_bond": h(L0_attn_W_bond).reshape(16),
        "w_dmg": h(L0_attn_W_dmg),
        "b_dmg": h(L0_attn_b_dmg).reshape(16),
        "w_dmg_out": h(L0_attn_W_dmg_out).reshape(16),
        "b_dmg_out": h(L0_attn_b_dmg_out).reshape(1),
        "w_cproj": h(L0_attn_W_cproj),
        "w_ffn": h(L0_ffn_weight),
        "b_ffn": h(L0_ffn_bias).reshape(C),
        "ident": ident, "sel01": sel01, "mask01": mask01, "onesrow": ones,
    }
    in_maps = [dict(common, x=np.ascontiguousarray(x[b])) for b in range(8)]
    res = run_bass_kernel_spmd(nc, in_maps, core_ids=list(range(8)))
    out = np.stack([res.results[b]["y"] for b in range(8)], axis=0)
    return out.astype(np.float32)


if __name__ == "__main__":
    build_nc()
    print("build ok")


# version 16
# speedup vs baseline: 1.3686x; 1.3686x over previous
import os
import numpy as np
import ml_dtypes
import concourse.bass as bass
import concourse.mybir as mybir
from concourse.bass_utils import run_bass_kernel_spmd
from concourse.ap import AP

F32 = mybir.dt.float32
BF16 = mybir.dt.bfloat16
FP8 = mybir.dt.float8e4
AF = mybir.ActivationFunctionType
ALU = mybir.AluOpType
AXX = mybir.AxisListType.X

T = 2048
C = 1024
NH = 16
NT = T // 128
PROW = 64 + T + 8
WROW = 256
NWBUF = 4

DEBUG = os.environ.get("KDEBUG") == "1"
PHASE_LIM = int(os.environ.get("KPHASE", "9"))


def mkap(t, offset, dims):
    return AP(t, int(offset), [list(d) for d in dims])


class Prog:
    ENGS = ("sync", "vector", "scalar", "tensor", "gpsimd")

    def __init__(self, nc):
        self.nc = nc
        self.plan = {e: [] for e in self.ENGS}
        self.sem = {}
        self.cnt = {}
        for name in ("SY", "GP", "PE", "DV", "AC", "CC",
                     "LDX0", "LDX1", "WW0", "WW1", "PQD0", "PQD1",
                     "SKW0", "SKW1", "STO", "BDL", "PT0", "PT1"):
            self.sem[name] = nc.alloc_semaphore(name)
            self.cnt[name] = 0

    def wait(self, eng, sname, val=None):
        v = self.cnt[sname] if val is None else val
        if v <= 0:
            return
        s = self.sem[sname]
        self.plan[eng].append(lambda e, s=s, v=v: e.wait_ge(s, v))

    def dma(self, dst, src, eng="sync", sem=None, **kw):
        sname = sem or ("GP" if eng == "gpsimd" else "SY")
        s = self.sem[sname]
        self.cnt[sname] += 16
        self.plan[eng].append(
            lambda e, d=dst, sr=src, s=s, kw=kw: e.dma_start(out=d, in_=sr, **kw).then_inc(s, 16)
        )
        return self.cnt[sname]

    CHAIN = {"vector": "DV", "scalar": "AC", "gpsimd": "CC"}

    def op(self, eng, meth, *args, inc=None, chain=True, **kw):
        sname = self.CHAIN.get(eng, inc)
        if sname is None:
            self.plan[eng].append(lambda e, m=meth, a=args, kw=kw: getattr(e, m)(*a, **kw))
            return None
        s = self.sem[sname]
        if eng in self.CHAIN and chain and self.cnt[sname] > 0:
            v = self.cnt[sname]
            self.plan[eng].append(lambda e, s=s, v=v: e.wait_ge(s, v))
        self.cnt[sname] += 1
        self.plan[eng].append(
            lambda e, m=meth, a=args, kw=kw, s=s: getattr(e, m)(*a, **kw).then_inc(s, 1)
        )
        return self.cnt[sname]

    def collective(self, kind, op, groups, ins, outs):
        s = self.sem["CC"]
        self.cnt["CC"] += 1
        self.plan["gpsimd"].append(
            lambda e, k=kind, o=op, g=groups, i=ins, u=outs, s=s:
                e.collective_compute(k, o, g, i, u).then_inc(s, 1)
        )
        return self.cnt["CC"]

    def emit(self):
        with self.nc.Block() as block:
            for ename in self.ENGS:
                items = self.plan[ename]
                getattr(block, ename)(lambda e, items=items: [f(e) for f in items])


def build_nc():
    nc = bass.Bass("TRN2", target_bir_lowering=False)

    x_in = nc.dram_tensor("x", [T, C], F32, kind="ExternalInput")
    w_disp = nc.dram_tensor("w_disp", [C, 256], F32, kind="ExternalInput")
    w_val = nc.dram_tensor("w_val", [C, C], F32, kind="ExternalInput")
    rel_pos = nc.dram_tensor("rel_pos", [64, 16], F32, kind="ExternalInput")
    w_strain = nc.dram_tensor("w_strain", [16, 16], F32, kind="ExternalInput")
    w_pos = nc.dram_tensor("w_pos", [16, 16], F32, kind="ExternalInput")
    w_bond = nc.dram_tensor("w_bond", [16], F32, kind="ExternalInput")
    w_dmg = nc.dram_tensor("w_dmg", [16, 16], F32, kind="ExternalInput")
    b_dmg = nc.dram_tensor("b_dmg", [16], F32, kind="ExternalInput")
    w_dmg_out = nc.dram_tensor("w_dmg_out", [16], F32, kind="ExternalInput")
    b_dmg_out = nc.dram_tensor("b_dmg_out", [1], F32, kind="ExternalInput")
    w_cproj = nc.dram_tensor("w_cproj", [C, C], F32, kind="ExternalInput")
    w_ffn = nc.dram_tensor("w_ffn", [C, C], F32, kind="ExternalInput")
    b_ffn = nc.dram_tensor("b_ffn", [C], F32, kind="ExternalInput")
    ident_in = nc.dram_tensor("ident", [128, 128], BF16, kind="ExternalInput")
    sel01_in = nc.dram_tensor("sel01", [128, 128], BF16, kind="ExternalInput")
    mask01_in = nc.dram_tensor("mask01", [128, 64], BF16, kind="ExternalInput")
    ones_in = nc.dram_tensor("onesrow", [1, 128], BF16, kind="ExternalInput")

    y_out = nc.dram_tensor("y", [T, C], F32, kind="ExternalOutput")

    p_dram = nc.dram_tensor("p_dram", [256 * PROW], BF16, kind="Internal")
    q_dram = nc.dram_tensor("q_dram", [256 * PROW], BF16, kind="Internal")
    w_dram = nc.dram_tensor("w_dram", [NWBUF * T * WROW], BF16, kind="Internal")
    x2_dram = nc.dram_tensor("x2_dram", [T * C], F32, kind="Internal")
    r_dram = nc.dram_tensor("r_dram", [64 * 16], F32, kind="Internal")
    sc_dram = nc.dram_tensor("sc_dram", [8], F32, kind="Internal")
    amax_dram = nc.dram_tensor("amax_dram", [128], F32, kind="Internal")
    cc_in = nc.dram_tensor("cc_in", [1, 128], F32, kind="Internal")
    cc_out = nc.dram_tensor("cc_out", [1, 128], F32, kind="Internal")

    dbg = {}
    if DEBUG:
        for nm, shp, dt in [
            ("p_fm", [128, 4096], BF16), ("q_fm", [128, 4096], BF16),
            ("val", [128, NT * 1024], BF16), ("yfm", [128, 16384], BF16),
            ("wun", [128, T], BF16), ("wtj", [128, 1024], BF16),
            ("attn", [128, 16384], BF16), ("x2d", [T * C], F32),
            ("xq", [128, 16384], BF16), ("wffnt", [128, 8192], BF16),
        ]:
            dbg[nm] = nc.dram_tensor("dbg_" + nm, shp, dt, kind="ExternalOutput")

    sb = nc.alloc_sbuf_tensor
    Wd = sb("Wd", [128, 8 * 256], BF16)
    Wv = sb("Wv", [128, 8 * 1024], BF16)
    Wc = sb("Wc", [128, 8 * 1024], BF16)
    BDs = sb("BDs", [128, 128], BF16)
    BDd = sb("BDd", [128, 128], BF16)
    wideP = sb("wideP", [128, 128], BF16)
    wideD = sb("wideD", [128, 128], BF16)
    sel01 = sb("sb_sel01", [128, 128], BF16)
    ident = sb("sb_ident", [128, 128], BF16)
    mask01 = sb("sb_mask01", [128, 64], BF16)
    onesr = sb("onesr", [1, 128], BF16)
    rcol = sb("rcol", [128, 8], F32)
    bdmgc = sb("bdmgc", [128, 1], F32)
    b0c = sb("b0c", [128, 1], F32)
    wbc = sb("wbc", [128, 1], F32)
    wdc = sb("wdc", [128, 1], F32)
    scal = sb("scal", [128, 4], F32)
    cm5 = sb("cm5", [128, 1], F32)
    s_tiny = sb("s_tiny", [1, 8], F32)

    xa = sb("xa", [128, 1024], F32)
    xb = sb("xb", [128, 1024], F32)
    y2scr = sb("y2scr", [128, 1024], F32)
    ytm = sb("ytm", [128, 1024], BF16)
    mcol = sb("mcol", [128, NT], F32)
    rstd = sb("rstd", [128, NT], F32)
    c1 = sb("c1", [128, 1], F32)
    c2 = sb("c2", [128, 1], F32)
    amx = sb("amx", [128, NT], F32)
    wamx = sb("wamx", [128, 8], F32)
    arow = sb("arow", [1, 128], F32)

    big_fm = sb("big_fm", [128, 16384], BF16)
    big_tm = sb("big_tm", [128, 16384], BF16)
    dspb = [sb(f"dspb{i}", [128, 512], BF16) for i in range(2)]
    pq_buf = [sb(f"pqb{i}", [128, 512], BF16) for i in range(2)]

    repP = [sb(f"repP{i}", [128, 2112], BF16) for i in range(2)]
    bcP = [sb(f"bcP{i}", [128, T], BF16) for i in range(2)]
    repQ = [sb(f"repQ{i}", [128, 2112], BF16) for i in range(2)]
    bcQ = [sb(f"bcQ{i}", [128, T], BF16) for i in range(2)]
    zsl = [sb(f"zsl{i}", [128, T], BF16) for i in range(2)]
    gsl = [sb(f"gsl{i}", [128, T], BF16) for i in range(2)]
    sgm = sb("sgm", [128, T], BF16)
    w_un = sb("w_un", [128, T], BF16)
    w_tj = [sb(f"wtjb{i}", [128, 1024], BF16) for i in range(2)]
    wsum = sb("wsum", [128, 16], F32)
    wrcp = sb("wrcp", [128, 16], F32)
    Vself = [sb(f"Vs{i}", [128, 2048], BF16) for i in range(2)]
    VcS = sb("VcS", [128, 2048], BF16)
    Vcorn = [VcS[0:64, :], VcS[64:128, :]]
    Wtil = sb("Wtil", [128, 3072], BF16)
    relT = sb("relT", [16, 64], F32)
    wpos_s = sb("wpos_s", [16, 16], F32)
    small = sb("small", [64, 16], F32)
    brow = sb("brow", [1, 1024], F32)
    brow_bf = sb("brow_bf", [1, 1024], BF16)

    psA = nc.alloc_psum_tensor("psA", [128, 512], F32)
    psB = nc.alloc_psum_tensor("psB", [128, 512], F32)
    ps_ctr = nc.alloc_psum_tensor("ps_ctr", [128, 2048], F32)
    ps_bf = nc.alloc_psum_tensor("ps_bf", [128, 1024], BF16)
    ps_w = nc.alloc_psum_tensor("ps_w", [128, 1024], BF16)

    P = Prog(nc)

    P.dma(ident[:], ident_in.ap())
    P.dma(sel01[:], sel01_in.ap())
    P.dma(mask01[:], mask01_in.ap())
    P.dma(onesr[:], ones_in.ap())
    P.dma(bdmgc[:], mkap(b_dmg, 0, [[0, 8], [1, 16], [0, 1]]))
    P.dma(wbc[:], mkap(w_bond, 0, [[0, 8], [1, 16], [0, 1]]))
    P.dma(wdc[:], mkap(w_dmg_out, 0, [[0, 8], [1, 16], [0, 1]]))
    P.dma(b0c[:], mkap(b_dmg_out, 0, [[0, 128], [0, 1]]))
    P.dma(relT[:], mkap(rel_pos, 0, [[1, 16], [16, 64]]), allow_slow_non_contiguous=True)
    P.dma(wpos_s[:], w_pos.ap())
    P.dma(brow[:], mkap(b_ffn, 0, [[0, 1], [1, C]]))
    P.dma(Wd[:].rearrange("p (c o) -> p c o", c=8),
          mkap(w_disp, 0, [[256, 128], [256 * 128, 8], [1, 256]]), eng="gpsimd")
    P.dma(Wv[:].rearrange("p (c o) -> p c o", c=8),
          mkap(w_val, 0, [[1024, 128], [1024 * 128, 8], [1, 1024]]), eng="gpsimd")
    P.dma(Wc[:].rearrange("p (c o) -> p c o", c=8),
          mkap(w_cproj, 0, [[1024, 128], [1024 * 128, 8], [1, 1024]]), eng="gpsimd")
    ld0, gp0 = P.cnt["SY"], P.cnt["GP"]

    P.wait("vector", "SY", ld0)
    P.wait("vector", "GP", gp0)
    P.op("vector", "memset", ytm[:], 0.0)
    P.op("vector", "memset", s_tiny[:], 0.0)
    P.op("vector", "memset", cm5[:], -5.0)
    P.op("vector", "tensor_scalar_mul", b0c[:], b0c[:], 0.5)
    P.op("vector", "memset", BDs[:], 0.0)
    P.op("vector", "memset", BDd[:], 0.0, inc="DV")
    bd_z = P.cnt["DV"]
    P.op("vector", "tensor_scalar_mul", wideP[:], sel01[:], wbc[:])
    P.op("vector", "tensor_scalar_mul", wideD[:], sel01[:], wdc[:], inc="DV")
    dv_wide = P.cnt["DV"]
    P.wait("gpsimd", "DV", bd_z)
    for blk in range(8):
        P.dma(BDs[16 * blk:16 * blk + 16, 16 * blk:16 * blk + 16], w_strain.ap(), eng="gpsimd", sem="BDL")
        P.dma(BDd[16 * blk:16 * blk + 16, 16 * blk:16 * blk + 16], w_dmg.ap(), eng="gpsimd", sem="BDL")
    gp_bd = P.cnt["BDL"]

    P.wait("tensor", "SY", ld0)
    P.op("tensor", "matmul", psA[0:64, 0:16], relT[:, 0:64], wpos_s[:],
         start=True, stop=True, inc="PE")
    P.wait("vector", "PE", P.cnt["PE"])
    P.op("vector", "tensor_copy", small[:], psA[0:64, 0:16], inc="DV")
    P.wait("sync", "DV", P.cnt["DV"])
    P.dma(mkap(r_dram, 0, [[16, 64], [1, 16]]), small[:])
    P.wait("sync", "SY")
    P.dma(rcol[:], mkap(r_dram, 0, [[16, 8], [1, 16], [128, 8]]),
          allow_slow_non_contiguous=True)
    P.wait("sync", "DV", dv_wide)
    for b in range(NWBUF):
        for k in range(4):
            P.dma(mkap(w_dram, (b * T + k * 512) * WROW, [[1024, 128], [1, 1024]]), ytm[:])
    for dr in (p_dram, q_dram):
        for k in range(4):
            P.dma(mkap(dr, k * 131072, [[1024, 128], [1, 1024]]), ytm[:])
        P.dma(mkap(dr, 524288, [[144, 128], [1, 144]]), ytm[:, 0:144])
    ld_const = P.cnt["SY"]

    def drain_all():
        for sn in ("SY", "LDX0", "LDX1", "WW0", "WW1", "PQD0", "PQD1", "SKW0", "SKW1", "STO", "BDL", "PT0", "PT1"):
            P.wait("sync", sn, P.cnt[sn])
        P.wait("gpsimd", "GP", P.cnt["GP"])

    def finish_early():
        P.wait("gpsimd", "SY", ld0)
        P.dma(mkap(y_out, 0, [[C, 128], [1, 128]]), sel01[:], eng="gpsimd")
        drain_all()
        P.emit()
        return nc

    if PHASE_LIM < 1:
        return finish_early()

    inv_c = 1.0 / C

    def ln_tile(buf, qt):
        P.op("vector", "tensor_reduce", c1[:], buf[:], AXX, ALU.add)
        P.op("vector", "tensor_scalar_mul", mcol[:, qt:qt + 1], c1[:], inv_c)
        P.op("vector", "scalar_tensor_tensor", ytm[:], buf[:], 0.0, buf[:],
             ALU.add, ALU.mult, accum_out=c1[:])
        P.op("vector", "tensor_scalar_mul", c1[:], c1[:], inv_c)
        P.op("vector", "tensor_tensor", c2[:], mcol[:, qt:qt + 1], mcol[:, qt:qt + 1], ALU.mult)
        P.op("vector", "scalar_tensor_tensor", c1[:], c1[:], 1e-5, c2[:],
             ALU.add, ALU.subtract, inc="DV")
        P.wait("scalar", "DV", P.cnt["DV"])
        P.op("scalar", "activation", c2[:], c1[:], AF.Sqrt, inc="AC")
        P.wait("vector", "AC", P.cnt["AC"])
        P.op("vector", "reciprocal", rstd[:, qt:qt + 1], c2[:])
        P.op("vector", "tensor_scalar", ytm[:], buf[:], mcol[:, qt:qt + 1],
             rstd[:, qt:qt + 1], ALU.subtract, ALU.mult, inc="DV")
        return P.cnt["DV"]

    xload_dv = {}
    ps_bf_free = 0
    ytm_pe = 0
    P.wait("vector", "SY", ld_const)
    for qt in range(NT):
        buf = xa if qt % 2 == 0 else xb
        if qt >= 2:
            P.wait("sync", "DV", xload_dv[qt - 2])
        lsl = "LDX" + str(qt % 2)
        P.dma(buf[:], mkap(x_in, qt * 128 * C, [[C, 128], [1, C]]), sem=lsl)
        P.wait("vector", lsl)
        P.wait("vector", "PE", ytm_pe)
        dv_n = ln_tile(buf, qt)
        xload_dv[qt] = dv_n
        P.wait("tensor", "DV", dv_n)
        P.wait("tensor", "DV", ps_bf_free)
        for c in range(8):
            P.op("tensor", "transpose", ps_bf[:, c * 128:c * 128 + 128],
                 ytm[:, c * 128:c * 128 + 128], ident[:],
                 inc=("PE" if c == 7 else None))
        ytm_pe = P.cnt["PE"]
        P.wait("vector", "PE", P.cnt["PE"])
        P.op("vector", "tensor_copy",
             mkap(big_fm, qt * 128, [[16384, 128], [2048, 8], [1, 128]]),
             ps_bf[:].rearrange("p (c t) -> p c t", c=8), inc="DV")
        ps_bf_free = P.cnt["DV"]

    yfm_dv = P.cnt["DV"]

    P.wait("tensor", "DV", yfm_dv)
    P.wait("tensor", "BDL", gp_bd)
    pqi = 0
    pq_done = {}
    di = 0
    dspb_free = [0, 0]
    for h in range(2):
        for tc in range(4):
            for c in range(8):
                P.op("tensor", "matmul", psA[:],
                     Wd[:, c * 256 + h * 128: c * 256 + h * 128 + 128],
                     mkap(big_fm, c * 2048 + tc * 512, [[16384, 128], [1, 512]]),
                     start=(c == 0), stop=(c == 7),
                     inc=("PE" if c == 7 else None))
            P.wait("vector", "PE", P.cnt["PE"])
            if dspb_free[di % 2]:
                P.wait("vector", "PE", dspb_free[di % 2])
            P.op("vector", "tensor_copy", dspb[di % 2][:], psA[:], inc="DV")
            P.wait("tensor", "DV", P.cnt["DV"])
            for (bd, dr, dbgk) in ((BDs, p_dram, "p_fm"), (BDd, q_dram, "q_fm")):
                P.wait("tensor", "DV", P.cnt["DV"])
                P.op("tensor", "matmul", psB[:], bd[:], dspb[di % 2][:],
                     start=True, stop=True, inc="PE")
                dspb_free[di % 2] = P.cnt["PE"]
                P.wait("vector", "PE", P.cnt["PE"])
                if pqi >= 2:
                    P.wait("vector", "PQD" + str(pqi % 2), pq_done[pqi - 2])
                P.op("vector", "tensor_copy", pq_buf[pqi % 2][:], psB[:], inc="DV")
                P.wait("sync", "DV", P.cnt["DV"])
                tok = P.dma(mkap(dr, h * 128 * PROW + 64 + tc * 512, [[PROW, 128], [1, 512]]),
                            pq_buf[pqi % 2][:], sem="PQD" + str(pqi % 2))
                if DEBUG:
                    tok = P.dma(mkap(dbg[dbgk], h * 2048 + tc * 512, [[4096, 128], [1, 512]]),
                                pq_buf[pqi % 2][:], sem="PQD" + str(pqi % 2))
                pq_done[pqi] = tok
                pqi += 1
    pq_w = (P.cnt["PQD0"], P.cnt["PQD1"])

    for qt in range(NT):
        P.wait("tensor", "DV", P.cnt["DV"])
        for half in range(2):
            ps = psA if half == 0 else psB
            for c in range(8):
                P.op("tensor", "matmul", ps[:],
                     mkap(big_fm, c * 2048 + qt * 128, [[16384, 128], [1, 128]]),
                     Wv[:, c * 1024 + half * 512: c * 1024 + half * 512 + 512],
                     start=(c == 0), stop=(c == 7),
                     inc=("PE" if c == 7 else None))
        P.wait("vector", "PE", P.cnt["PE"])
        P.op("vector", "tensor_copy", big_tm[:, qt * 1024: qt * 1024 + 512], psA[:])
        P.op("vector", "tensor_copy", big_tm[:, qt * 1024 + 512: qt * 1024 + 1024], psB[:],
             inc="DV")
    val_dv = P.cnt["DV"]

    if DEBUG:
        P.wait("sync", "DV", val_dv)
        P.dma(dbg["val"].ap(), big_tm[:])
        P.dma(dbg["yfm"].ap(), big_fm[:])

    for oc in range(8):
        P.wait("sync", "DV", P.cnt["DV"])
        lsl = "LDX" + str(oc % 2)
        wbuf = xa if oc % 2 == 0 else xb
        P.dma(wbuf[:], mkap(w_ffn, oc * 128 * C, [[C, 128], [1, C]]), sem=lsl)
        P.wait("vector", lsl)
        P.op("vector", "tensor_reduce", wamx[:, oc:oc + 1], wbuf[:], AXX, ALU.max,
             apply_absolute_value=True, inc="DV")
    P.op("vector", "tensor_reduce", c1[:], wamx[:], AXX, ALU.max,
         apply_absolute_value=True, inc="DV")
    P.wait("sync", "DV", P.cnt["DV"])
    P.dma(mkap(amax_dram, 0, [[1, 128], [0, 1]]), c1[:])
    P.wait("sync", "SY")
    P.dma(arow[:], mkap(amax_dram, 0, [[0, 1], [1, 128]]))
    P.wait("vector", "SY")
    P.op("vector", "tensor_reduce", s_tiny[0:1, 1:2], arow[:], AXX, ALU.max,
         apply_absolute_value=True)
    P.op("vector", "tensor_scalar_max", s_tiny[0:1, 1:2], s_tiny[0:1, 1:2], 1e-12)
    P.op("vector", "reciprocal", s_tiny[0:1, 3:4], s_tiny[0:1, 1:2])
    P.op("vector", "tensor_scalar_mul", s_tiny[0:1, 3:4], s_tiny[0:1, 3:4], 224.0, inc="DV")
    P.wait("sync", "DV", P.cnt["DV"])
    P.dma(mkap(sc_dram, 4, [[0, 1], [1, 1]]), s_tiny[0:1, 3:4])
    P.wait("sync", "SY")
    P.dma(scal[:, 1:2], mkap(sc_dram, 4, [[0, 128], [1, 1]]))
    sw2_tok = P.cnt["SY"]
    P.wait("vector", "SY", sw2_tok)
    wprep_dv = P.cnt["DV"]

    if PHASE_LIM < 2:
        return finish_early()

    state = {
        "zfree": [0, 0], "gfree": [0, 0], "wtj_free": [0, 0], "v_free": [0, 0],
        "sig_ac": 0, "comb_dv": 0, "wun_ta_pe": 0, "ps_bf_free": ps_bf_free,
        "skew_tok": [0] * NH, "wtil_dv": 0, "av_pe": [0] * NH,
        "attn_evac_dv": 0,
    }
    state["ps_bf_free"] = yfm_dv

    def av_chain(n):
        vslot = n % 2
        P.wait("tensor", "SY", state["skew_tok"][n])
        for qg in range(4):
            if not (n == 0 and qg == 0):
                P.wait("tensor", "DV", state["wtil_dv"])
            for k in range(4):
                qt = qg * 4 + k
                P.op("tensor", "transpose", ps_w[:, k * 192: k * 192 + 128],
                     Vself[vslot][:, qt * 128: qt * 128 + 128], ident[:])
                if qt > 0:
                    P.op("tensor", "transpose", ps_w[:, k * 192 + 128: k * 192 + 192],
                         Vcorn[vslot][:, qt * 128: qt * 128 + 128], ident[0:64, 0:64])
            P.op("tensor", "engine_nop", inc="PE") if False else None
            P.op("tensor", "transpose", ps_w[:, (3) * 192 + 128 + 0: (3) * 192 + 192]
                 if qg * 4 + 3 > 0 else ps_w[:, 0:128],
                 Vcorn[vslot][:, (qg * 4 + 3) * 128: (qg * 4 + 3) * 128 + 128]
                 if qg * 4 + 3 > 0 else Vself[vslot][:, 0:128],
                 ident[0:64, 0:64] if qg * 4 + 3 > 0 else ident[:], inc="PE")
            tr_pe = P.cnt["PE"]
            P.wait("vector", "PE", tr_pe)
            P.op("vector", "tensor_copy", Wtil[:, qg * 768: qg * 768 + 768],
                 ps_w[:, 0:768], inc="DV")
            state["wtil_dv"] = P.cnt["DV"]
            P.wait("tensor", "DV", state["wtil_dv"])
            if n >= 2:
                P.wait("tensor", "DV", state["attn_evac_dv"])
            for k in range(4):
                qt = qg * 4 + k
                P.op("tensor", "matmul", psA[0:64, k * 128: k * 128 + 128],
                     big_tm[:, qt * 1024 + 64 * n: qt * 1024 + 64 * n + 64],
                     Wtil[:, qg * 768 + k * 192: qg * 768 + k * 192 + 128],
                     start=True, stop=(qt == 0))
                if qt > 0:
                    P.op("tensor", "matmul", psA[0:64, k * 128: k * 128 + 64],
                         big_tm[:, (qt - 1) * 1024 + 64 * n: (qt - 1) * 1024 + 64 * n + 64],
                         Wtil[:, qg * 768 + k * 192 + 128: qg * 768 + k * 192 + 192],
                         start=False, stop=True)
            P.op("tensor", "transpose", ps_w[:, 0:0 + 128] if False else psA[0:1, 0:1],
                 None, None) if False else None
            pass
        return

    def av_chain(n):
        vslot = n % 2
        P.wait("tensor", "SKW" + str(vslot), state["skew_tok"][n])
        for qg in range(4):
            if not (n == 0 and qg == 0):
                P.wait("tensor", "DV", state["wtil_dv"])
            ops = []
            for k in range(4):
                qt = qg * 4 + k
                ops.append(("transpose", ps_w[:, k * 192: k * 192 + 128],
                            Vself[vslot][:, qt * 128: qt * 128 + 128], ident[:]))
                ops.append(("transpose", ps_w[:, k * 192 + 128: k * 192 + 192],
                            Vcorn[vslot][:, qt * 128: qt * 128 + 128],
                            ident[64 * vslot: 64 * vslot + 64, 64 * vslot: 64 * vslot + 64]))
            for j, o in enumerate(ops):
                P.op("tensor", o[0], *o[1:], inc=("PE" if j == len(ops) - 1 else None))
            P.wait("vector", "PE", P.cnt["PE"])
            P.op("vector", "tensor_copy", Wtil[:, qg * 768: qg * 768 + 768],
                 ps_w[:, 0:768], inc="DV")
            state["wtil_dv"] = P.cnt["DV"]
            P.wait("tensor", "DV", state["wtil_dv"])
            if state["attn_evac_dv"]:
                P.wait("tensor", "DV", state["attn_evac_dv"])
            mms = []
            for k in range(4):
                qt = qg * 4 + k
                mms.append((psA[0:64, k * 128: k * 128 + 128],
                            big_tm[:, qt * 1024 + 64 * n: qt * 1024 + 64 * n + 64],
                            Wtil[:, qg * 768 + k * 192: qg * 768 + k * 192 + 128],
                            True, qt == 0))
                if qt > 0:
                    mms.append((psA[0:64, k * 128: k * 128 + 64],
                                big_tm[:, (qt - 1) * 1024 + 64 * n: (qt - 1) * 1024 + 64 * n + 64],
                                Wtil[:, qg * 768 + k * 192 + 128: qg * 768 + k * 192 + 192],
                                False, True))
            for j, m in enumerate(mms):
                P.op("tensor", "matmul", m[0], m[1], m[2], start=m[3], stop=m[4],
                     inc=("PE" if j == len(mms) - 1 else None))
            av_pe = P.cnt["PE"]
            P.wait("vector", "PE", av_pe)
            r0 = 64 * (n % 2)
            P.op("vector", "tensor_copy",
                 big_fm[r0:r0 + 64, (n // 2) * 2048 + qg * 512: (n // 2) * 2048 + qg * 512 + 512],
                 psA[0:64, :], inc="DV")
            state["attn_evac_dv"] = P.cnt["DV"]
        state["v_free"][vslot] = P.cnt["PE"]

    def pt_load(hp, X):
        ptsem = "PT" + str((hp * 2 + X) % 2)
        rep, bc, dr = (repQ, bcQ, q_dram) if X == 0 else (repP, bcP, p_dram)
        for (i, n) in ((0, 2 * hp), (1, 2 * hp + 1)):
            P.dma(rep[i][:], mkap(dr, (16 * n) * PROW, [[1, 8], [PROW, 16], [1, 2112]]),
                  eng="gpsimd", sem=ptsem)
            P.dma(bc[i][:], mkap(dr, (16 * n) * PROW + 64, [[0, 8], [PROW, 16], [1, T]]),
                  eng="gpsimd", sem=ptsem)
        return ptsem, P.cnt[ptsem]

    P.wait("gpsimd", "PQD0", pq_w[0])
    P.wait("gpsimd", "PQD1", pq_w[1])
    P.wait("gpsimd", "SY", ld_const)
    pt_tok = {(0, 0): pt_load(0, 0), (0, 1): pt_load(0, 1)}

    for hp in range(8):
        for X in range(2):
            rep, bc = (repQ, bcQ) if X == 0 else (repP, bcP)
            wide = wideD if X == 0 else wideP
            ptsem, my_pt = pt_tok[(hp, X)]
            for sub in range(2):
                for g in range(8):
                    i = (X * 16 + sub * 8 + g) % 2
                    P.wait("vector", ptsem, my_pt)
                    P.wait("vector", "AC", state["zfree"][i])
                    P.op("vector", "tensor_tensor", zsl[i][:],
                         rep[sub][:, 1 + 8 * g: 1 + 8 * g + T], bc[sub][:],
                         ALU.subtract, inc="DV")
                    P.wait("scalar", "DV", P.cnt["DV"])
                    P.wait("scalar", "PE", state["gfree"][i])
                    bias = rcol[:, g:g + 1] if X == 1 else bdmgc[:]
                    state["zfree"][i] = P.op("scalar", "activation", gsl[i][:], zsl[i][:],
                                             AF.Gelu, bias=bias, inc="AC")
                    P.wait("tensor", "AC", state["zfree"][i])
                    if sub == 0 and g == 0:
                        if X == 0 and hp > 0:
                            P.wait("tensor", "DV", state["comb_dv"])
                        if X == 1:
                            P.wait("tensor", "AC", state["sig_ac"])
                    for ch in range(4):
                        P.op("tensor", "matmul",
                             ps_ctr[64 * sub:64 * sub + 64, ch * 512: ch * 512 + 512],
                             wide[:, 56 - 8 * g: 56 - 8 * g + 64],
                             gsl[i][:, ch * 512: ch * 512 + 512],
                             start=(g == 0), stop=(g == 7),
                             inc=("PE" if ch == 3 else None))
                    state["gfree"][i] = P.cnt["PE"]
            if hp < 7:
                P.wait("gpsimd", "DV", P.cnt["DV"])
                pt_tok[(hp + 1, X)] = pt_load(hp + 1, X)
            if X == 0:
                P.wait("scalar", "PE", P.cnt["PE"])
                if hp > 0:
                    P.wait("scalar", "DV", state["comb_dv"])
                state["sig_ac"] = P.op("scalar", "activation", sgm[:], ps_ctr[:],
                                       AF.Tanh, bias=b0c[:], scale=0.5, inc="AC")

        P.wait("vector", "PE", P.cnt["PE"])
        P.wait("vector", "AC", state["sig_ac"])
        P.op("vector", "scalar_tensor_tensor", sgm[:], sgm[:], -5.0, ps_ctr[:],
             ALU.mult, ALU.add, inc="DV")
        state["comb_dv"] = P.cnt["DV"]
        P.wait("scalar", "DV", state["comb_dv"])
        P.wait("scalar", "PE", state["wun_ta_pe"])
        if DEBUG and state.get("dbg_sy"):
            P.wait("scalar", "SY", state["dbg_sy"])
        e_ac = P.op("scalar", "activation", w_un[:], sgm[:], AF.Exp,
                    bias=cm5[:], inc="AC")
        P.wait("vector", "AC", e_ac)
        P.op("vector", "tensor_tensor", w_un[:, 0:64], w_un[:, 0:64], mask01[:],
             ALU.mult, inc="DV")
        pair_mask_dv = P.cnt["DV"]

        for sub in range(2):
            n = 2 * hp + sub
            wslot = n % 2
            P.wait("tensor", "DV", pair_mask_dv)
            P.wait("tensor", "DV", state["ps_bf_free"])
            for qt in range(NT):
                P.op("tensor", "transpose", ps_bf[:, qt * 64: qt * 64 + 64],
                     w_un[64 * sub: 64 * sub + 64, qt * 128: qt * 128 + 128],
                     ident[64 * sub: 64 * sub + 64, 64 * sub: 64 * sub + 64],
                     inc=("PE" if qt == NT - 1 else None))
            ta_pe = P.cnt["PE"]
            if sub == 1:
                state["wun_ta_pe"] = ta_pe
            P.wait("vector", "PE", ta_pe)
            P.wait("vector", "WW" + str(wslot), state["wtj_free"][wslot])
            P.op("vector", "tensor_copy", w_tj[wslot][:], ps_bf[:], inc="DV")
            state["ps_bf_free"] = P.cnt["DV"]
            P.op("vector", "tensor_reduce", wsum[:],
                 w_tj[wslot][:].rearrange("p (a j) -> p a j", a=16), AXX, ALU.add)
            P.op("vector", "reciprocal", wrcp[:], wsum[:])
            P.op("vector", "tensor_tensor",
                 w_tj[wslot][:].rearrange("p (a j) -> p a j", a=16),
                 w_tj[wslot][:].rearrange("p (a j) -> p a j", a=16),
                 mkap(wrcp, 0, [[16, 128], [1, 16], [0, 64]]),
                 ALU.mult, inc="DV")
            norm_dv = P.cnt["DV"]
            if DEBUG and n == 0:
                P.wait("sync", "DV", pair_mask_dv)
                P.dma(dbg["wun"].ap(), w_un[:])
                P.wait("sync", "DV", norm_dv)
                P.dma(dbg["wtj"].ap(), w_tj[wslot][:])
                state["dbg_sy"] = P.cnt["SY"]
            P.wait("sync", "DV", norm_dv)
            wb = (n % NWBUF) * T * WROW
            state["wtj_free"][wslot] = P.dma(
                mkap(w_dram, wb + 128, [[WROW, 128], [128 * WROW, 16], [1, 64]]),
                w_tj[wslot][:].rearrange("p (a j) -> p a j", a=16),
                sem="WW" + str(wslot))
            if n >= 2:
                av_chain(n - 2)
            vslot = n % 2
            P.wait("sync", "PE", state["v_free"][vslot])
            P.wait("sync", "WW" + str(wslot), state["wtj_free"][wslot])
            P.dma(Vself[vslot][:],
                  mkap(w_dram, wb + 191, [[255, 128], [32768, 16], [1, 128]]),
                  sem="SKW" + str(vslot))
            state["skew_tok"][n] = P.dma(
                Vcorn[vslot][:], mkap(w_dram, wb + 63, [[255, 64], [32768, 16], [1, 128]]),
                sem="SKW" + str(vslot))

    av_chain(NH - 2)
    av_chain(NH - 1)
    attn_dv = P.cnt["DV"]

    if DEBUG:
        P.wait("sync", "DV", attn_dv)
        P.dma(dbg["attn"].ap(), big_fm[:])

    if PHASE_LIM < 4:
        return finish_early()

    x2_w = {}
    for qt in range(NT):
        buf = xa if qt % 2 == 0 else xb
        if qt >= 2:
            P.wait("sync", "DV", x2_w[qt - 2][0])
        else:
            P.wait("sync", "DV", wprep_dv)
        lsl = "LDX" + str(qt % 2)
        P.dma(buf[:], mkap(x_in, qt * 128 * C, [[C, 128], [1, C]]), sem=lsl)
        ld_tok = (lsl, P.cnt[lsl])
        if qt >= 2:
            P.wait("tensor", "DV", x2_w[qt - 2][0])
        P.wait("tensor", "DV", attn_dv)
        pA = psA[:] if qt % 2 == 0 else ps_ctr[:, 0:512]
        pB = psB[:] if qt % 2 == 0 else ps_ctr[:, 512:1024]
        for half in range(2):
            ps = pA if half == 0 else pB
            for c in range(8):
                P.op("tensor", "matmul", ps,
                     big_fm[:, c * 2048 + qt * 128: c * 2048 + qt * 128 + 128],
                     Wc[:, c * 1024 + half * 512: c * 1024 + half * 512 + 512],
                     start=(c == 0), stop=(c == 7),
                     inc=("PE" if c == 7 else None))
        P.wait("vector", "PE", P.cnt["PE"])
        P.wait("vector", ld_tok[0], ld_tok[1])
        if qt >= 1:
            P.wait("vector", "STO", x2_w[qt - 1][1])
        P.op("vector", "scalar_tensor_tensor", y2scr[:, 0:512], pA, 0.0,
             buf[:, 0:512], ALU.add, ALU.add)
        P.op("vector", "scalar_tensor_tensor", y2scr[:, 512:1024], pB, 0.0,
             buf[:, 512:1024], ALU.add, ALU.add, inc="DV")
        dv_tok = P.cnt["DV"]
        P.wait("scalar", "DV", dv_tok)
        tok = P.dma(mkap(x2_dram, qt * 128 * C, [[C, 128], [1, C]]), y2scr[:],
                    sem="STO", eng="scalar")
        P.op("vector", "tensor_reduce", c1[:], y2scr[:], AXX, ALU.add)
        P.op("vector", "tensor_scalar_mul", mcol[:, qt:qt + 1], c1[:], inv_c)
        P.op("vector", "scalar_tensor_tensor", ytm[:], y2scr[:], 0.0, y2scr[:],
             ALU.add, ALU.mult, accum_out=c1[:])
        P.op("vector", "tensor_scalar_mul", c1[:], c1[:], inv_c)
        P.op("vector", "tensor_tensor", c2[:], mcol[:, qt:qt + 1], mcol[:, qt:qt + 1], ALU.mult)
        P.op("vector", "scalar_tensor_tensor", c1[:], c1[:], 1e-5, c2[:],
             ALU.add, ALU.subtract, inc="DV")
        P.wait("scalar", "DV", P.cnt["DV"])
        P.op("scalar", "activation", c2[:], c1[:], AF.Sqrt, inc="AC")
        P.wait("vector", "AC", P.cnt["AC"])
        P.op("vector", "reciprocal", rstd[:, qt:qt + 1], c2[:])
        P.op("vector", "tensor_reduce", c1[:], y2scr[:], AXX, ALU.max)
        P.op("vector", "tensor_tensor", c1[:], c1[:], mcol[:, qt:qt + 1], ALU.subtract)
        P.op("vector", "tensor_reduce", c2[:], y2scr[:], AXX, ALU.min)
        P.op("vector", "tensor_tensor", c2[:], mcol[:, qt:qt + 1], c2[:], ALU.subtract)
        P.op("vector", "tensor_tensor", c1[:], c1[:], c2[:], ALU.max)
        P.op("vector", "tensor_tensor", amx[:, qt:qt + 1], c1[:], rstd[:, qt:qt + 1],
             ALU.mult, inc="DV")
        x2_w[qt] = (P.cnt["DV"], tok)
    x2_done = P.cnt["STO"]

    if DEBUG or os.environ.get("KDEBUG2") == "1":
        P.wait("sync", "STO", x2_done)
        dbg_x2 = dbg.get("x2d") or nc.dram_tensor("dbg_x2d", [T * C], F32, kind="ExternalOutput")
        P.dma(dbg_x2.ap(), x2_dram.ap())

    if PHASE_LIM < 5:
        return finish_early()


    P.op("vector", "tensor_reduce", c1[:], amx[:], AXX, ALU.max,
         apply_absolute_value=True, inc="DV")
    P.wait("sync", "DV", P.cnt["DV"])
    P.dma(mkap(amax_dram, 0, [[1, 128], [0, 1]]), c1[:])
    P.wait("sync", "SY")
    P.dma(arow[:], mkap(amax_dram, 0, [[0, 1], [1, 128]]))
    P.wait("vector", "SY")
    P.op("vector", "tensor_reduce", s_tiny[0:1, 0:1], arow[:], AXX, ALU.max,
         apply_absolute_value=True, inc="DV")
    xam_dv = P.cnt["DV"]

    P.wait("gpsimd", "DV", xam_dv)
    P.dma(cc_in.ap(), arow[:], eng="gpsimd")
    P.wait("gpsimd", "GP")
    cc_tok = P.collective("AllReduce", ALU.max, [[0, 1, 2, 3, 4, 5, 6, 7]],
                          [cc_in.ap()], [cc_out.ap()])
    P.wait("gpsimd", "CC", cc_tok)
    P.dma(arow[:], cc_out.ap(), eng="gpsimd")
    gp_amax = P.cnt["GP"]

    ps_bf_tok = 0
    for oc in range(8):
        buf = xa if oc % 2 == 0 else xb
        P.wait("sync", "DV", P.cnt["DV"])
        lsl = "LDX" + str(oc % 2)
        P.dma(buf[:], mkap(w_ffn, oc * 128 * C, [[C, 128], [1, C]]), sem=lsl)
        P.wait("vector", lsl)
        w8v = (xb if oc % 2 == 0 else xa).bitcast(FP8)
        P.wait("vector", "PE", P.cnt["PE"])
        P.op("vector", "tensor_scalar_mul", w8v[:, 0:1024], buf[:], scal[:, 1:2])
        P.op("vector", "tensor_copy", ytm[:], w8v[:, 0:1024], inc="DV")
        P.wait("tensor", "DV", P.cnt["DV"])
        if ps_bf_tok:
            P.wait("tensor", "DV", ps_bf_tok)
        for dc in range(8):
            P.op("tensor", "transpose", ps_bf[:, dc * 128: dc * 128 + 128],
                 ytm[:, dc * 128: dc * 128 + 128], ident[:],
                 inc=("PE" if dc == 7 else None))
        P.wait("vector", "PE", P.cnt["PE"])
        P.op("vector", "tensor_copy",
             mkap(big_tm, oc * 128, [[16384, 128], [1024, 8], [1, 128]]),
             ps_bf[:].rearrange("p (dc o) -> p dc o", dc=8), inc="DV")
        ps_bf_tok = P.cnt["DV"]
    wffnt_dv = P.cnt["DV"]

    P.wait("vector", "GP", gp_amax)
    P.op("vector", "tensor_reduce", s_tiny[0:1, 0:1], arow[:], AXX, ALU.max,
         apply_absolute_value=True)
    P.op("vector", "tensor_scalar_max", s_tiny[0:1, 0:1], s_tiny[0:1, 0:1], 1e-12)
    P.op("vector", "reciprocal", s_tiny[0:1, 2:3], s_tiny[0:1, 0:1])
    P.op("vector", "tensor_scalar_mul", s_tiny[0:1, 2:3], s_tiny[0:1, 2:3], 224.0)
    P.op("vector", "tensor_tensor", s_tiny[0:1, 4:5], s_tiny[0:1, 0:1],
         s_tiny[0:1, 1:2], ALU.mult)
    P.op("vector", "reciprocal", s_tiny[0:1, 5:6], s_tiny[0:1, 4:5])
    P.op("vector", "tensor_scalar_mul", s_tiny[0:1, 4:5], s_tiny[0:1, 4:5], 1.0 / 50176.0)
    P.op("vector", "tensor_scalar_mul", s_tiny[0:1, 5:6], s_tiny[0:1, 5:6], 50176.0, inc="DV")
    P.wait("sync", "DV", P.cnt["DV"])
    P.dma(mkap(sc_dram, 0, [[0, 1], [1, 8]]), s_tiny[:])
    P.wait("sync", "SY")
    P.dma(scal[:, 0:1], mkap(sc_dram, 2, [[0, 128], [1, 1]]))
    P.dma(scal[:, 2:4], mkap(sc_dram, 4 + 0, [[0, 128], [1, 2]]))
    scal_tok = P.cnt["SY"]

    P.wait("vector", "SY", scal_tok)
    P.op("vector", "tensor_scalar_mul", brow_bf[:], brow[:], s_tiny[0:1, 5:6], inc="DV")


    if DEBUG:
        P.wait("sync", "DV", wffnt_dv)
        P.dma(dbg["wffnt"].ap(), big_tm[:, 0:8192])

    xq_dv = 0
    P.wait("vector", "STO", x2_done)
    for qt in range(NT):
        buf = xa if qt % 2 == 0 else xb
        P.wait("sync", "DV", P.cnt["DV"])
        lsl = "LDX" + str(qt % 2)
        P.dma(buf[:], mkap(x2_dram, qt * 128 * C, [[C, 128], [1, C]]), sem=lsl)
        P.wait("vector", lsl)
        P.op("vector", "tensor_scalar", y2scr[:], buf[:], mcol[:, qt:qt + 1],
             rstd[:, qt:qt + 1], ALU.subtract, ALU.mult)
        w8v = (xb if qt % 2 == 0 else xa).bitcast(FP8)
        P.wait("vector", "PE", P.cnt["PE"])
        P.op("vector", "tensor_scalar_mul", w8v[:, 0:1024], y2scr[:], scal[:, 0:1])
        P.op("vector", "tensor_copy", ytm[:], w8v[:, 0:1024], inc="DV")
        P.wait("tensor", "DV", P.cnt["DV"])
        P.wait("tensor", "DV", ps_bf_tok)
        for dc in range(8):
            P.op("tensor", "transpose", ps_bf[:, dc * 128: dc * 128 + 128],
                 ytm[:, dc * 128: dc * 128 + 128], ident[:],
                 inc=("PE" if dc == 7 else None))
        P.wait("vector", "PE", P.cnt["PE"])
        P.op("vector", "tensor_copy",
             mkap(big_fm, qt * 128, [[16384, 128], [2048, 8], [1, 128]]),
             ps_bf[:].rearrange("p (dc t) -> p dc t", dc=8), inc="DV")
        ps_bf_tok = P.cnt["DV"]
    xq_dv = P.cnt["DV"]
    if DEBUG or os.environ.get("KDEBUG2") == "1":
        dbg_xq = dbg.get("xq") or nc.dram_tensor("dbg_xq", [128, 16384], BF16, kind="ExternalOutput")
        dbg_wf = dbg.get("wffnt") or nc.dram_tensor("dbg_wffnt", [128, 8192], BF16, kind="ExternalOutput")
        P.wait("sync", "DV", xq_dv)
        P.dma(dbg_xq.ap(), big_fm[:])
        P.dma(dbg_wf.ap(), big_tm[:, 0:8192])

    if PHASE_LIM < 6:
        return finish_early()

    out_toks = []
    prev_dv = 0
    q6dv = {}
    for qt in range(NT):
        buf = xa if qt % 2 == 0 else xb
        if qt >= 2:
            P.wait("sync", "DV", prev_dv)
        else:
            P.wait("sync", "DV", xq_dv)
        lsl = "LDX" + str(qt % 2)
        P.dma(buf[:], mkap(x2_dram, qt * 128 * C, [[C, 128], [1, C]]), sem=lsl)
        ld_tok = (lsl, P.cnt[lsl])
        P.wait("tensor", "DV", xq_dv)
        P.wait("tensor", "DV", wffnt_dv)
        if qt >= 2:
            P.wait("tensor", "DV", q6dv[qt - 2])
        pA = psA[:] if qt % 2 == 0 else ps_ctr[:, 0:512]
        pB = psB[:] if qt % 2 == 0 else ps_ctr[:, 512:1024]
        for half in range(2):
            ps = pA if half == 0 else pB
            for dc in range(8):
                P.op("tensor", "matmul", ps,
                     big_fm[:, dc * 2048 + qt * 128: dc * 2048 + qt * 128 + 128],
                     big_tm[:, dc * 1024 + half * 512: dc * 1024 + half * 512 + 512],
                     start=(dc == 0), stop=False)
            P.op("tensor", "matmul", ps, onesr[:],
                 brow_bf[0:1, half * 512: half * 512 + 512],
                 start=False, stop=True, inc="PE")
        P.wait("vector", "PE", P.cnt["PE"])
        P.wait("vector", ld_tok[0], ld_tok[1])
        if out_toks:
            P.wait("vector", "STO", out_toks[-1])
        P.op("vector", "scalar_tensor_tensor", y2scr[:, 0:512], pA, scal[:, 2:3],
             buf[:, 0:512], ALU.mult, ALU.add)
        P.op("vector", "scalar_tensor_tensor", y2scr[:, 512:1024], pB, scal[:, 2:3],
             buf[:, 512:1024], ALU.mult, ALU.add, inc="DV")
        q6dv[qt] = P.cnt["DV"]
        prev_dv = P.cnt["DV"]
        P.wait("scalar", "DV", prev_dv)
        out_toks.append(P.dma(mkap(y_out, qt * 128 * C, [[C, 128], [1, C]]), y2scr[:],
                              sem="STO", eng="scalar"))

    drain_all()

    P.emit()
    return nc


def host_constants():
    ident = np.eye(128, dtype=ml_dtypes.bfloat16)
    sel01 = np.zeros((128, 128), dtype=ml_dtypes.bfloat16)
    for p in range(128):
        sel01[p, 56 + p // 16] = 1.0
    mask01 = np.ones((128, 64), dtype=ml_dtypes.bfloat16)
    for jj in range(64):
        for t in range(64):
            if jj < 63 - t:
                mask01[jj, t] = 0.0
                mask01[jj + 64, t] = 0.0
    ones = np.ones((1, 128), dtype=ml_dtypes.bfloat16)
    return ident, sel01, mask01, ones


_NC_CACHE = {}


def kernel(L0_attn_x, L0_attn_W_disp, L0_attn_W_val, L0_attn_rel_pos_emb,
           L0_attn_W_strain, L0_attn_W_pos, L0_attn_W_bond, L0_attn_W_dmg,
           L0_attn_b_dmg, L0_attn_W_dmg_out, L0_attn_b_dmg_out, L0_attn_W_cproj,
           L0_ffn_weight, L0_ffn_bias):
    if "nc" not in _NC_CACHE:
        _NC_CACHE["nc"] = build_nc()
    nc = _NC_CACHE["nc"]
    ident, sel01, mask01, ones = host_constants()

    def h(v):
        a = np.asarray(v)
        return np.ascontiguousarray(a.astype(np.float32, copy=False))

    x = h(L0_attn_x)
    common = {
        "w_disp": h(L0_attn_W_disp),
        "w_val": h(L0_attn_W_val),
        "rel_pos": h(L0_attn_rel_pos_emb),
        "w_strain": h(L0_attn_W_strain),
        "w_pos": h(L0_attn_W_pos),
        "w_bond": h(L0_attn_W_bond).reshape(16),
        "w_dmg": h(L0_attn_W_dmg),
        "b_dmg": h(L0_attn_b_dmg).reshape(16),
        "w_dmg_out": h(L0_attn_W_dmg_out).reshape(16),
        "b_dmg_out": h(L0_attn_b_dmg_out).reshape(1),
        "w_cproj": h(L0_attn_W_cproj),
        "w_ffn": h(L0_ffn_weight),
        "b_ffn": h(L0_ffn_bias).reshape(C),
        "ident": ident, "sel01": sel01, "mask01": mask01, "onesrow": ones,
    }
    in_maps = [dict(common, x=np.ascontiguousarray(x[b])) for b in range(8)]
    res = run_bass_kernel_spmd(nc, in_maps, core_ids=list(range(8)))
    out = np.stack([res.results[b]["y"] for b in range(8)], axis=0)
    return out.astype(np.float32)


if __name__ == "__main__":
    build_nc()
    print("build ok")
